# version 12
# baseline (speedup 1.0000x reference)
import numpy as np
from contextlib import ExitStack

import concourse.bass as bass
import concourse.mybir as mybir
from concourse.alu_op_type import AluOpType as OP
from concourse.bass_utils import run_bass_kernel_spmd

F32 = mybir.dt.float32
BF16 = mybir.dt.bfloat16
I32 = mybir.dt.int32
AF = mybir.ActivationFunctionType

D = 1024
KC = 8
SIG = 256
NTT = SIG // 128
NSLOT = 8
NWID = 64


class Eng:
    def __init__(self, prog, name):
        self.prog = prog
        self.name = name
        self.n = 0
        self.marks = {}
        self.sem = None

    def ins(self, f, mark=None):
        self.n += 1
        if self.prog.emit:
            instr = f()
            if self.n in self.prog.signals[self.name]:
                instr.then_inc(self.sem, 1)
        if mark is not None:
            if self.prog.emit:
                assert self.marks.get(mark) == self.n, \
                    f"pass drift: {self.name} {mark}: {self.marks.get(mark)} vs {self.n}"
            self.marks[mark] = self.n
        return self.n

    def mark_here(self, key):
        if self.prog.emit:
            assert self.marks.get(key) == self.n
        self.marks[key] = self.n


class Prog:
    def __init__(self):
        self.emit = False
        self.engs = {}
        self.waited_keys = {n: set() for n in ("T", "V", "S", "G")}
        self.signals = {n: {} for n in ("T", "V", "S", "G")}
        self.dma = {}

    def eng(self, name):
        if name not in self.engs:
            self.engs[name] = Eng(self, name)
        return self.engs[name]

    def wait_mark(self, waiter_raw, producer, key):
        if not self.emit:
            self.waited_keys[producer.name].add(key)
            return
        idx = producer.marks[key]
        waiter_raw.wait_ge(producer.sem, self.signals[producer.name][idx])

    def finalize_pass1(self):
        for name, keys in self.waited_keys.items():
            marks = self.engs[name].marks
            idxs = sorted({marks[k] for k in keys})
            self.signals[name] = {i: r + 1 for r, i in enumerate(idxs)}
        for e in self.engs.values():
            e.n = 0
        self.dma = {}
        self.emit = True


def _newton_rsqrt(V, nc, out, vin, nw, add_c, mul_c, q, wide, tmp, iters=4, mark=None):
    if mul_c != 1.0:
        V.ins(lambda: nc.vector.tensor_scalar(q[:, :], vin, mul_c, add_c,
                                              op0=OP.mult, op1=OP.add))
    else:
        V.ins(lambda: nc.vector.tensor_scalar(q[:, :], vin, add_c, None, op0=OP.add))
    qi, oi = q[:, :].bitcast(I32), wide[:, :].bitcast(I32)
    V.ins(lambda: nc.vector.tensor_scalar(oi, qi, 1, None, op0=OP.arith_shift_right))
    V.ins(lambda: nc.vector.tensor_scalar(oi, oi, -1, None, op0=OP.bitwise_xor))
    V.ins(lambda: nc.vector.tensor_scalar(oi, oi, 0x5f3759df + 1, None, op0=OP.add))
    for _ in range(iters):
        V.ins(lambda: nc.vector.tensor_tensor(tmp[:, :], wide[:, :], wide[:, :], op=OP.mult))
        V.ins(lambda: nc.vector.tensor_tensor(tmp[:, :], tmp[:, :], q[:, :], op=OP.mult))
        V.ins(lambda: nc.vector.tensor_scalar(tmp[:, :], tmp[:, :], -0.5, 1.5,
                                              op0=OP.mult, op1=OP.add))
        V.ins(lambda: nc.vector.tensor_tensor(wide[:, :], wide[:, :], tmp[:, :], op=OP.mult))
    V.ins(lambda: nc.vector.tensor_copy(out, wide[:, 0:nw]), mark=mark)


def build(T):
    NS = T // SIG
    nc = bass.Bass("TRN2", target_bir_lowering=False, detect_race_conditions=False)

    xd = nc.dram_tensor("x", [T, D], F32, kind="ExternalInput").ap()
    win_d = nc.dram_tensor("w_in", [D, 3 * D], F32, kind="ExternalInput").ap()
    wout_d = nc.dram_tensor("w_out", [D, D], F32, kind="ExternalInput").ap()
    wr_d = nc.dram_tensor("w_r", [D, D], F32, kind="ExternalInput").ap()
    wk_d = nc.dram_tensor("w_k", [D, D], F32, kind="ExternalInput").ap()
    wv_d = nc.dram_tensor("w_v", [D, D], F32, kind="ExternalInput").ap()
    wn_d = nc.dram_tensor("wn", [D], F32, kind="ExternalInput").ap()
    ll_d = nc.dram_tensor("ll", [D], F32, kind="ExternalInput").ap()
    mur_d = nc.dram_tensor("mur", [D], F32, kind="ExternalInput").ap()
    muk_d = nc.dram_tensor("muk", [D], F32, kind="ExternalInput").ap()
    out_d = nc.dram_tensor("out", [T, D], F32, kind="ExternalOutput").ap()
    wr_s = nc.dram_tensor("wr_s", [D, D], BF16, kind="Internal").ap()
    wk_s = nc.dram_tensor("wk_s", [D, D], BF16, kind="Internal").ap()
    wv_s = nc.dram_tensor("wv_s", [D, D], BF16, kind="Internal").ap()

    with ExitStack() as ctx:
        def sb(name, shape, dtype):
            return ctx.enter_context(nc.sbuf_tensor(name, shape, dtype))

        def psum_(name, shape):
            return ctx.enter_context(nc.psum_tensor(name, shape, F32))

        reg = sb("reg", [128, NSLOT, D], F32)
        w_in = sb("w_in_sb", [128, KC, 3 * D], BF16)
        w_out = sb("w_out_sb", [128, KC, D], BF16)
        wrt = sb("wrt", [128, KC, D], BF16)
        wkt = sb("wkt", [128, KC, D], BF16)
        wvt = sb("wvt", [128, KC, D], BF16)
        xnf = sb("xnf", [128, KC, SIG], BF16)
        xn2f = sb("xn2f", [128, 2, KC, SIG], BF16)
        stok = sb("stok", [128, D], BF16)
        a_t = sb("a_t", [128, KC, SIG], F32)
        rgate = sb("rgate", [128, KC, SIG], F32)
        egel = sb("egel", [128, 2, SIG], F32)
        ub = sb("ub", [128, SIG], F32)
        ytmp = sb("ytmp", [128, 2, SIG], F32)
        ybf = sb("ybf", [128, KC, SIG], BF16)
        ycar = sb("ycar", [128, KC], F32)
        xrf = sb("xrf", [128, KC, SIG], BF16)
        xkf = sb("xkf", [128, KC, SIG], BF16)
        xdb = sb("xdb", [128, 2, SIG], BF16)
        e1b = sb("e1b", [128, 2, SIG], F32)
        kact = sb("kact", [128, KC, SIG], BF16)
        rsig = sb("rsig", [128, 3, 512], F32)
        vecs = sb("vecs", [128, 40], F32)
        stat = sb("stat", [128, NTT * 12], F32)
        mv = sb("mv", [128, NTT, 2], F32)
        vvv = sb("vvv", [128, NWID], F32)
        msc = vvv[:, 8:12]
        sc1 = vvv[:, 12:12 + NTT]
        sc2 = vvv[:, 16:16 + NTT]
        sc3 = vvv[:, 20:20 + NTT]
        scq = sb("scq", [128, NWID], F32)
        scw = sb("scw", [128, NWID], F32)
        sct = sb("sct", [128, NWID], F32)

        def vc(base, j):
            return vecs[:, base + j:base + j + 1]
        B_WN, B_LAMH, B_NLAMH, B_MUR1, B_MUK1 = 0, 8, 16, 24, 32

        pa = [psum_("pa0", [128, SIG]), psum_("pa1", [128, SIG])]
        po = [psum_("po0", [128, 512]), psum_("po1", [128, 512]), psum_("po2", [128, 512])]
        prv = [psum_("prv0", [128, 512]), psum_("prv1", [128, 512]), psum_("prv2", [128, 512])]

        s_ldx = ctx.enter_context(nc.semaphore("s_ldx"))
        s_dmat = ctx.enter_context(nc.semaphore("s_dmat"))
        s_st = ctx.enter_context(nc.semaphore("s_st"))
        s_ldg = ctx.enter_context(nc.semaphore("s_ldg"))
        sem_T = ctx.enter_context(nc.semaphore("sem_T"))
        sem_G = ctx.enter_context(nc.semaphore("sem_G"))
        sem_V = ctx.enter_context(nc.semaphore("sem_V"))
        sem_S = ctx.enter_context(nc.semaphore("sem_S"))

        P = Prog()
        TE, V, S, G = P.eng("T"), P.eng("V"), P.eng("S"), P.eng("G")

        pa_bank = {}
        po_bank = {}
        prv_bank = {}
        npa = npo = nprv = 0
        for s in range(NS + 3):
            sA, sW, sF = s - 1, s - 2, s - 3
            if 0 <= sW < NS:
                for H in range(4):
                    po_bank[(s, H)] = npo % 3
                    npo += 1
            if 0 <= sA < NS:
                for m in range(24):
                    pa_bank[(s, m)] = npa % 2
                    npa += 1
            if 0 <= sF < NS:
                for j in range(KC):
                    pa_bank[(s, 24 + j)] = npa % 2
                    npa += 1
                for H in range(4):
                    prv_bank[(s, "r", H)] = nprv % 3
                    nprv += 1
                for H in range(4):
                    prv_bank[(s, "v", H)] = nprv % 3
                    nprv += 1

        def emit_program():
            em = P.emit
            c = P.dma
            for k in ("ldx", "dmat", "st", "ldg"):
                c[k] = 0
            pa_free = []
            po_free = []
            prv_free = []
            xn_dmat_at = {}
            wt_dmat = [0]

            def dma(eng_raw, sem, key, out_ap_f, in_ap_f, transpose=False):
                c[key] += 16
                if em:
                    eng_raw.dma_start(out_ap_f(), in_ap_f(), transpose=transpose).then_inc(sem, 16)

            def dwait(eng_raw, sem, val):
                if em and val > 0:
                    eng_raw.wait_ge(sem, val)

            def gate(eng_raw, lst, nbank):
                g = len(lst)
                if g >= nbank:
                    who, key = lst[g - nbank]
                    P.wait_mark(eng_raw, who, key)

            if em:
                for dvec, base in [(wn_d, B_WN), (ll_d, B_LAMH), (mur_d, B_MUR1), (muk_d, B_MUK1)]:
                    nc.gpsimd.dma_start(vecs[:, base:base + 8],
                                        dvec.rearrange("(a b) -> b a", b=128),
                                        allow_slow_non_contiguous=True).then_inc(s_ldg, 16)
            c["ldg"] += 64
            for kc in range(KC):
                dma(nc.gpsimd, s_ldg, "ldg", lambda kc=kc: w_in[:, kc, :],
                    lambda kc=kc: win_d[kc * 128:(kc + 1) * 128, :])
            ldg_win = c["ldg"]
            for ec in range(KC):
                dma(nc.gpsimd, s_ldg, "ldg", lambda ec=ec: w_out[:, ec, :],
                    lambda ec=ec: wout_d[ec * 128:(ec + 1) * 128, :])
            ldg_wout = c["ldg"]
            dma(nc.gpsimd, s_ldg, "ldg", lambda: wr_s[:, :], lambda: wr_d[:, :])
            dma(nc.gpsimd, s_ldg, "ldg", lambda: wk_s[:, :], lambda: wk_d[:, :])
            dma(nc.gpsimd, s_ldg, "ldg", lambda: wv_s[:, :], lambda: wv_d[:, :])
            ldg_all = c["ldg"]

            dwait(nc.scalar, s_ldg, 64)
            S.ins(lambda: nc.scalar.activation(vecs[:, B_LAMH:B_LAMH + 8],
                                               vecs[:, B_LAMH:B_LAMH + 8], AF.Sigmoid), mark="lam")

            for kc in range(KC):
                dwait(nc.gpsimd, s_ldg, 64 + 16 * (kc + 1))
                G.ins(lambda kc=kc: nc.gpsimd.tensor_scalar(w_in[:, kc, :], w_in[:, kc, :],
                                                            vc(B_WN, kc), None, op0=OP.mult),
                      mark=(("fold", kc)))
            P.wait_mark(nc.vector, S, "lam")
            V.ins(lambda: nc.vector.tensor_scalar(vecs[:, B_NLAMH:B_NLAMH + 8],
                                                  vecs[:, B_LAMH:B_LAMH + 8], -1.0, None, op0=OP.mult))
            V.ins(lambda: nc.vector.tensor_scalar(vecs[:, B_MUR1:B_MUR1 + 8],
                                                  vecs[:, B_MUR1:B_MUR1 + 8], -1.0, None, op0=OP.add))
            V.ins(lambda: nc.vector.tensor_scalar(vecs[:, B_MUK1:B_MUK1 + 8],
                                                  vecs[:, B_MUK1:B_MUK1 + 8], -1.0, None, op0=OP.add))
            V.ins(lambda: nc.vector.memset(ycar[:, :], 0.0))
            V.ins(lambda: nc.vector.memset(vvv[:, :], 1.0))
            V.ins(lambda: nc.vector.memset(scq[:, :], 1.0))
            V.ins(lambda: nc.vector.memset(scw[:, :], 1.0))
            V.ins(lambda: nc.vector.memset(sct[:, :], 1.0))
            for j in range(KC - 1):
                V.ins(lambda j=j: nc.vector.memset(xn2f[:, 1, j, SIG - 1:SIG], 0.0))
            V.ins(lambda: nc.vector.memset(xn2f[:, 1, KC - 1, SIG - 1:SIG], 0.0), mark="prep")
            P.wait_mark(nc.tensor, V, "prep")
            P.wait_mark(nc.tensor, G, ("fold", KC - 1))

            def ln_aggr_norm(tag, sg, scale_out, mul_c, add_c, extra_r1):
                for tt in range(NTT):
                    V.ins(lambda: nc.vector.memset(scq[:, 0:1], 0.0))
                    V.ins(lambda tt=tt: nc.vector.bn_aggr(mv[:, tt, :],
                                                          stat[:, tt * 12:tt * 12 + 12]),
                          mark=(tag + "ag", sg, tt))
                P.wait_mark(nc.vector, V, (tag + "ag", sg, NTT - 1))
                for tt in range(NTT):
                    V.ins(lambda tt=tt: nc.vector.tensor_copy(vvv[:, tt:tt + 1], mv[:, tt, 1:2]),
                          mark=(tag + "cp", sg, tt))
                P.wait_mark(nc.vector, V, (tag + "cp", sg, NTT - 1))
                if extra_r1:
                    _newton_rsqrt(V, nc, sc1, vvv[:, :], NTT, 1e-5, 1.0, scq, scw, sct)
                _newton_rsqrt(V, nc, scale_out[:, :], vvv[:, :], NTT, add_c, mul_c, scq, scw, sct,
                              mark=(tag + "nw", sg))
                P.wait_mark(nc.vector, V, (tag + "nw", sg))
                V.ins(lambda: nc.vector.tensor_tensor(msc[:, 0:NTT], mv[:, :, 0], scale_out[:, :],
                                                      op=OP.mult))
                if extra_r1:
                    V.ins(lambda: nc.vector.tensor_tensor(msc[:, NTT:2 * NTT], mv[:, :, 0],
                                                          sc1, op=OP.mult))
                    V.ins(lambda: nc.vector.tensor_scalar(sct[:, 0:NTT], sc1, 1.0, None,
                                                          op0=OP.add), mark=(tag + "ms", sg))
                else:
                    V.ins(lambda: nc.vector.tensor_copy(msc[:, 2:3], msc[:, 0:1]),
                          mark=(tag + "ms", sg))
                P.wait_mark(nc.vector, V, (tag + "ms", sg))

            for s in range(NS + 3):
                sL, sA, sW, sF = s, s - 1, s - 2, s - 3
                doL = sL < NS
                doA = 0 <= sA < NS
                doW = 0 <= sW < NS
                doF = 0 <= sF < NS

                if s == 0:
                    for tt in range(NTT):
                        gt = tt
                        dma(nc.sync, s_ldx, "ldx", lambda gt=gt: reg[:, gt % NSLOT, :],
                            lambda gt=gt: xd[gt * 128:(gt + 1) * 128, :])
                    dwait(nc.sync, s_ldx, c["ldx"])

                if doW:
                    if sW == 0:
                        dwait(nc.tensor, s_ldg, ldg_wout)
                    P.wait_mark(nc.tensor, V, ("y", sW))
                    for tc in range(NTT):
                        for half in range(2):
                            H = 2 * tc + half
                            gate(nc.tensor, po_free, 3)
                            po_free.append((V, ("pofree", s, H)))
                            b = po_bank[(s, H)]
                            for ec in range(KC):
                                TE.ins(lambda tc=tc, half=half, ec=ec, b=b: nc.tensor.matmul(
                                    po[b][:, :], ybf[:, ec, tc * 128:(tc + 1) * 128],
                                    w_out[:, ec, half * 512:(half + 1) * 512],
                                    start=(ec == 0), stop=(ec == KC - 1)),
                                    mark=(("po", s, H) if ec == KC - 1 else None))
                    TE.mark_here(("wout_done", s))

                if doA:
                    dwait(nc.tensor, s_dmat, xn_dmat_at[sA])
                    for m in range(24):
                        gate(nc.tensor, pa_free, 2)
                        pa_free.append((S, ("pafree", s, m)))
                        b = pa_bank[(s, m)]
                        for kc in range(KC):
                            TE.ins(lambda kc=kc, m=m, b=b: nc.tensor.matmul(
                                pa[b][:, :], w_in[:, kc, m * 128:(m + 1) * 128], xnf[:, kc, :],
                                start=(kc == 0), stop=(kc == KC - 1)),
                                mark=(("pa", s, m) if kc == KC - 1 else None))
                    TE.mark_here(("proj_done", sA))

                if doF:
                    if sF == 0:
                        dwait(nc.tensor, s_dmat, wt_dmat[0])
                    P.wait_mark(nc.tensor, V, ("mix", sF))
                    for j in range(KC):
                        gate(nc.tensor, pa_free, 2)
                        pa_free.append((S, ("pafree", s, 24 + j)))
                        b = pa_bank[(s, 24 + j)]
                        for cc in range(KC):
                            TE.ins(lambda cc=cc, j=j, b=b: nc.tensor.matmul(
                                pa[b][:, :], wkt[:, cc, j * 128:(j + 1) * 128], xkf[:, cc, :],
                                start=(cc == 0), stop=(cc == KC - 1)),
                                mark=(("pa", s, 24 + j) if cc == KC - 1 else None))
                    TE.mark_here(("k_done", s))
                    for tc in range(NTT):
                        for half in range(2):
                            H = 2 * tc + half
                            gate(nc.tensor, prv_free, 3)
                            prv_free.append((S, ("prvfree_r", s, H)))
                            b = prv_bank[(s, "r", H)]
                            for cc in range(KC):
                                TE.ins(lambda cc=cc, tc=tc, half=half, b=b: nc.tensor.matmul(
                                    prv[b][:, :], xrf[:, cc, tc * 128:(tc + 1) * 128],
                                    wrt[:, cc, half * 512:(half + 1) * 512],
                                    start=(cc == 0), stop=(cc == KC - 1)),
                                    mark=(("prv_r", s, H) if cc == KC - 1 else None))
                    TE.mark_here(("r_done", s))
                    P.wait_mark(nc.tensor, G, ("kact", sF, KC - 1))
                    for tc in range(NTT):
                        for half in range(2):
                            H = 2 * tc + half
                            gate(nc.tensor, prv_free, 3)
                            prv_free.append((V, ("prvfree_v", s, H)))
                            b = prv_bank[(s, "v", H)]
                            for dc in range(KC):
                                TE.ins(lambda dc=dc, tc=tc, half=half, b=b: nc.tensor.matmul(
                                    prv[b][:, :], kact[:, dc, tc * 128:(tc + 1) * 128],
                                    wvt[:, dc, half * 512:(half + 1) * 512],
                                    start=(dc == 0), stop=(dc == KC - 1)),
                                    mark=(("prv_v", s, H) if dc == KC - 1 else None))
                    TE.mark_here(("v_done", s))

                if doW:
                    for tc in range(NTT):
                        slot = (sW * NTT + tc) % NSLOT
                        for half in range(2):
                            H = 2 * tc + half
                            P.wait_mark(nc.vector, TE, ("po", s, H))
                            b = po_bank[(s, H)]
                            V.ins(lambda slot=slot, half=half, b=b: nc.vector.tensor_tensor(
                                reg[:, slot, half * 512:(half + 1) * 512],
                                reg[:, slot, half * 512:(half + 1) * 512],
                                po[b][:, :], op=OP.add), mark=("pofree", s, H))
                        V.ins(lambda slot=slot, tc=tc: nc.vector.bn_stats(
                            stat[:, tc * 12:tc * 12 + 6], reg[:, slot, 0:512]))
                        V.ins(lambda slot=slot, tc=tc: nc.vector.bn_stats(
                            stat[:, tc * 12 + 6:tc * 12 + 12], reg[:, slot, 512:1024]))
                    ln_aggr_norm("L2", sW, sc3, 1.0, 1e-5, extra_r1=False)
                    for tt in range(NTT):
                        slot = (sW * NTT + tt) % NSLOT
                        dwait(nc.vector, s_dmat, c["dmat"])
                        V.ins(lambda slot=slot, tt=tt: nc.vector.tensor_scalar(
                            stok[:, :], reg[:, slot, :], sc3[:, tt:tt + 1], msc[:, tt:tt + 1],
                            op0=OP.mult, op1=OP.subtract), mark=("ln2", sW, tt))

                        P.wait_mark(nc.sync, V, ("ln2", sW, tt))
                        if sW >= 1:
                            P.wait_mark(nc.sync, V, ("mix", sW - 1))
                        dma(nc.sync, s_dmat, "dmat",
                            lambda tt=tt, par=sW % 2: xn2f[:, par, :, tt * 128:(tt + 1) * 128],
                            lambda: stok[:, :], transpose=True)
                        dwait(nc.sync, s_dmat, c["dmat"])

                if doA:
                    if sA >= 1:
                        P.wait_mark(nc.scalar, V, ("scan", sA - 1))
                    for j in range(KC):
                        P.wait_mark(nc.scalar, TE, ("pa", s, j))
                        b = pa_bank[(s, j)]
                        S.ins(lambda j=j, b=b: nc.scalar.activation(
                            a_t[:, j, :], pa[b][:, :], AF.Sigmoid), mark=("pafree", s, j))
                        S.mark_here(("a", sA, j))
                    if sA >= 1:
                        P.wait_mark(nc.scalar, V, ("u1", sA - 1, KC - 1))
                    for j in range(KC):
                        P.wait_mark(nc.scalar, TE, ("pa", s, 8 + j))
                        b = pa_bank[(s, 8 + j)]
                        S.ins(lambda j=j, b=b: nc.scalar.activation(
                            rgate[:, j, :], pa[b][:, :], AF.Sigmoid), mark=("pafree", s, 8 + j))
                        S.mark_here(("r", sA, j))
                    for j in range(KC):
                        J = KC * sA + j
                        P.wait_mark(nc.scalar, TE, ("pa", s, 16 + j))
                        if J >= 2:
                            sp, jp = divmod(J - 2, KC)
                            P.wait_mark(nc.scalar, V, ("vp", sp, jp))
                        b = pa_bank[(s, 16 + j)]
                        S.ins(lambda j=j, b=b: nc.scalar.activation(
                            egel[:, j % 2, :], pa[b][:, :], AF.Gelu), mark=("pafree", s, 16 + j))
                        S.mark_here(("e", sA, j))

                if doA:
                    for j in range(KC):
                        P.wait_mark(nc.vector, S, ("a", sA, j))
                        V.ins(lambda j=j: nc.vector.tensor_scalar(ub[:, :], a_t[:, j, :],
                                                                  vc(B_NLAMH, j), vc(B_LAMH, j),
                                                                  op0=OP.mult, op1=OP.add))
                        P.wait_mark(nc.vector, S, ("r", sA, j))
                        V.ins(lambda j=j: nc.vector.tensor_tensor(ub[:, :], ub[:, :],
                                                                  rgate[:, j, :], op=OP.mult),
                              mark=("u1", sA, j))
                        P.wait_mark(nc.vector, S, ("e", sA, j))
                        V.ins(lambda j=j: nc.vector.tensor_tensor(ub[:, :], ub[:, :],
                                                                  egel[:, j % 2, :], op=OP.mult),
                              mark=("vp", sA, j))
                        V.ins(lambda j=j: nc.vector.tensor_tensor_scan(
                            ytmp[:, j % 2, :], a_t[:, j, :], ub[:, :], ycar[:, j:j + 1],
                            op0=OP.mult, op1=OP.add), mark=("scanop", sA, j))
                        if j == 1 and doW:
                            P.wait_mark(nc.vector, TE, ("wout_done", s))
                        if j >= 1:
                            V.ins(lambda j=j: nc.vector.tensor_copy(
                                ycar[:, j - 1:j], ytmp[:, (j - 1) % 2, SIG - 1:SIG]))
                            V.ins(lambda j=j: nc.vector.tensor_copy(
                                ybf[:, j - 1, :], ytmp[:, (j - 1) % 2, :]))
                    P.wait_mark(nc.vector, V, ("scanop", sA, KC - 1))
                    V.ins(lambda: nc.vector.tensor_copy(ycar[:, KC - 1:KC],
                                                        ytmp[:, (KC - 1) % 2, SIG - 1:SIG]))
                    V.ins(lambda: nc.vector.tensor_copy(ybf[:, KC - 1, :], ytmp[:, (KC - 1) % 2, :]),
                          mark=("y", sA))
                    V.mark_here(("scan", sA))

                if doL:
                    dwait(nc.vector, s_ldx, c["ldx"])
                    for tt in range(NTT):
                        slot = (sL * NTT + tt) % NSLOT
                        V.ins(lambda slot=slot, tt=tt: nc.vector.bn_stats(
                            stat[:, tt * 12:tt * 12 + 6], reg[:, slot, 0:512]))
                        V.ins(lambda slot=slot, tt=tt: nc.vector.bn_stats(
                            stat[:, tt * 12 + 6:tt * 12 + 12], reg[:, slot, 512:1024]))
                    ln_aggr_norm("L1", sL, sc2, 1.000001, 1e-11, extra_r1=True)
                    for tt in range(NTT):
                        slot = (sL * NTT + tt) % NSLOT
                        dwait(nc.vector, s_dmat, c["dmat"])
                        V.ins(lambda slot=slot, tt=tt: nc.vector.tensor_scalar(
                            stok[:, :], reg[:, slot, :], sc2[:, tt:tt + 1], msc[:, tt:tt + 1],
                            op0=OP.mult, op1=OP.subtract), mark=("ln1", sL, tt))
                        V.ins(lambda slot=slot, tt=tt: nc.vector.tensor_scalar(
                            reg[:, slot, :], reg[:, slot, :], sct[:, tt:tt + 1],
                            msc[:, NTT + tt:NTT + tt + 1], op0=OP.mult, op1=OP.subtract))

                        P.wait_mark(nc.sync, V, ("ln1", sL, tt))
                        if sL >= 1:
                            P.wait_mark(nc.sync, TE, ("proj_done", sL - 1))
                        dma(nc.sync, s_dmat, "dmat",
                            lambda tt=tt: xnf[:, :, tt * 128:(tt + 1) * 128], lambda: stok[:, :],
                            transpose=True)
                        dwait(nc.sync, s_dmat, c["dmat"])
                    xn_dmat_at[sL] = c["dmat"]
                    if s == 1:
                        dwait(nc.sync, s_ldg, ldg_all)
                        for scr, dst in [(wr_s, wrt), (wk_s, wkt), (wv_s, wvt)]:
                            dma(nc.sync, s_dmat, "dmat", lambda dst=dst: dst[:, :, :],
                                lambda scr=scr: scr[:, :], transpose=True)
                            dwait(nc.sync, s_dmat, c["dmat"])
                        wt_dmat[0] = c["dmat"]

                if doF:
                    for j in range(KC):
                        J = KC * sF + j
                        P.wait_mark(nc.scalar, TE, ("pa", s, 24 + j))
                        if J >= 2:
                            sp, jp = divmod(J - 2, KC)
                            P.wait_mark(nc.scalar, G, ("kact", sp, jp))
                        b = pa_bank[(s, 24 + j)]
                        S.ins(lambda j=j, b=b: nc.scalar.activation(
                            e1b[:, j % 2, :], pa[b][:, :], AF.Relu), mark=("pafree", s, 24 + j))
                        S.mark_here(("e1", sF, j))
                    for H in range(4):
                        P.wait_mark(nc.scalar, TE, ("prv_r", s, H))
                        if 4 * sF + H >= 3:
                            hp, hh = divmod(4 * sF + H - 3, 4)
                            P.wait_mark(nc.scalar, V, ("rv", hp, hh))
                        b = prv_bank[(s, "r", H)]
                        S.ins(lambda H=H, b=b: nc.scalar.activation(
                            rsig[:, H % 3, :], prv[b][:, :], AF.Sigmoid), mark=("prvfree_r", s, H))
                        S.mark_here(("rs", sF, H))

                if doF:
                    if sF >= 1:
                        P.wait_mark(nc.gpsimd, TE, ("v_done", s - 1))
                    for j in range(KC):
                        P.wait_mark(nc.gpsimd, S, ("e1", sF, j))
                        G.ins(lambda j=j: nc.gpsimd.tensor_tensor(
                            kact[:, j, :], e1b[:, j % 2, :], e1b[:, j % 2, :], op=OP.mult),
                            mark=("kact", sF, j))
                if doW:
                    dwait(nc.vector, s_dmat, c["dmat"])
                    if doF:
                        P.wait_mark(nc.vector, TE, ("k_done", s))
                        P.wait_mark(nc.vector, TE, ("r_done", s))
                    par = sW % 2
                    for j in range(KC):
                        xv = xn2f[:, par, j, :]
                        V.ins(lambda xv=xv, j=j: nc.vector.tensor_tensor(
                            xdb[:, j % 2, 1:SIG], xv[:, 1:SIG], xv[:, 0:SIG - 1], op=OP.subtract))
                        V.ins(lambda xv=xv, j=j, par=par: nc.vector.tensor_tensor(
                            xdb[:, j % 2, 0:1], xv[:, 0:1], xn2f[:, 1 - par, j, SIG - 1:SIG],
                            op=OP.subtract))
                        if j >= 1:
                            V.ins(lambda j=j, par=par: nc.vector.scalar_tensor_tensor(
                                xrf[:, j - 1, :], xdb[:, (j - 1) % 2, :], vc(B_MUR1, j - 1),
                                xn2f[:, par, j - 1, :], op0=OP.mult, op1=OP.add))
                            V.ins(lambda j=j, par=par: nc.vector.scalar_tensor_tensor(
                                xkf[:, j - 1, :], xdb[:, (j - 1) % 2, :], vc(B_MUK1, j - 1),
                                xn2f[:, par, j - 1, :], op0=OP.mult, op1=OP.add))
                    V.ins(lambda par=par: nc.vector.scalar_tensor_tensor(
                        xrf[:, KC - 1, :], xdb[:, (KC - 1) % 2, :], vc(B_MUR1, KC - 1),
                        xn2f[:, par, KC - 1, :], op0=OP.mult, op1=OP.add))
                    V.ins(lambda par=par: nc.vector.scalar_tensor_tensor(
                        xkf[:, KC - 1, :], xdb[:, (KC - 1) % 2, :], vc(B_MUK1, KC - 1),
                        xn2f[:, par, KC - 1, :], op0=OP.mult, op1=OP.add), mark=("mix", sW))

                if doF:
                    for H in range(4):
                        tc, half = divmod(H, 2)
                        slot = (sF * NTT + tc) % NSLOT
                        P.wait_mark(nc.vector, TE, ("prv_v", s, H))
                        P.wait_mark(nc.vector, S, ("rs", sF, H))
                        b = prv_bank[(s, "v", H)]
                        V.ins(lambda b=b, H=H: nc.vector.tensor_tensor(
                            rsig[:, H % 3, :], prv[b][:, :], rsig[:, H % 3, :], op=OP.mult))
                        V.ins(lambda slot=slot, half=half, H=H: nc.vector.tensor_tensor(
                            reg[:, slot, half * 512:(half + 1) * 512],
                            reg[:, slot, half * 512:(half + 1) * 512], rsig[:, H % 3, :],
                            op=OP.add), mark=("rv", sF, H))
                        V.mark_here(("prvfree_v", s, H))
                        if half == 1:
                            V.mark_here(("x2", sF, tc))

                if doF:
                    for tt in range(NTT):
                        gt = sF * NTT + tt
                        P.wait_mark(nc.sync, V, ("x2", sF, tt))
                        dma(nc.sync, s_st, "st", lambda gt=gt: out_d[gt * 128:(gt + 1) * 128, :],
                            lambda gt=gt: reg[:, gt % NSLOT, :])
                    dwait(nc.sync, s_st, c["st"])
                if s + 1 < NS:
                    for tt in range(NTT):
                        gt = (s + 1) * NTT + tt
                        dma(nc.sync, s_ldx, "ldx", lambda gt=gt: reg[:, gt % NSLOT, :],
                            lambda gt=gt: xd[gt * 128:(gt + 1) * 128, :])
                    dwait(nc.sync, s_ldx, c["ldx"])

            dwait(nc.sync, s_st, 16 * 2 * NS)

        emit_program()
        P.finalize_pass1()
        TE.sem, V.sem, S.sem, G.sem = sem_T, sem_V, sem_S, sem_G
        emit_program()
    return nc


def kernel(L0_attn_x, L0_attn_w_norm, L0_attn_W_in, L0_attn_lambda_log, L0_attn_W_out,
           L0_ffn_W_r, L0_ffn_W_k, L0_ffn_W_v, L0_ffn_mu_r, L0_ffn_mu_k):
    B, T, D_ = L0_attn_x.shape
    assert (B, T, D_) == (8, 2048, 1024)
    nc = build(T)
    w = {
        "w_in": np.ascontiguousarray(L0_attn_W_in, dtype=np.float32),
        "w_out": np.ascontiguousarray(L0_attn_W_out, dtype=np.float32),
        "w_r": np.ascontiguousarray(L0_ffn_W_r, dtype=np.float32),
        "w_k": np.ascontiguousarray(L0_ffn_W_k, dtype=np.float32),
        "w_v": np.ascontiguousarray(L0_ffn_W_v, dtype=np.float32),
        "wn": np.ascontiguousarray(L0_attn_w_norm, dtype=np.float32),
        "ll": np.ascontiguousarray(L0_attn_lambda_log, dtype=np.float32),
        "mur": np.ascontiguousarray(L0_ffn_mu_r, dtype=np.float32),
        "muk": np.ascontiguousarray(L0_ffn_mu_k, dtype=np.float32),
    }
    in_maps = [dict(w, x=np.ascontiguousarray(L0_attn_x[i], dtype=np.float32)) for i in range(B)]
    res = run_bass_kernel_spmd(nc, in_maps, core_ids=list(range(B)))
    return np.stack([res.results[i]["out"] for i in range(B)], axis=0)


# version 15
# speedup vs baseline: 1.4507x; 1.4507x over previous
import numpy as np
from contextlib import ExitStack

import concourse.bass as bass
import concourse.mybir as mybir
from concourse.alu_op_type import AluOpType as OP
from concourse.bass_utils import run_bass_kernel_spmd

F32 = mybir.dt.float32
BF16 = mybir.dt.bfloat16
I32 = mybir.dt.int32
AF = mybir.ActivationFunctionType

D = 1024
KC = 8
SIG = 256
NTT = SIG // 128
NSLOT = 8
NWID = 64


class Eng:
    def __init__(self, prog, name):
        self.prog = prog
        self.name = name
        self.n = 0
        self.marks = {}
        self.sem = None

    def ins(self, f, mark=None):
        self.n += 1
        if self.prog.emit:
            instr = f()
            if self.n in self.prog.signals[self.name]:
                instr.then_inc(self.sem, 1)
        if mark is not None:
            if self.prog.emit:
                assert self.marks.get(mark) == self.n, \
                    f"pass drift: {self.name} {mark}: {self.marks.get(mark)} vs {self.n}"
            self.marks[mark] = self.n
        return self.n

    def mark_here(self, key):
        if self.prog.emit:
            assert self.marks.get(key) == self.n
        self.marks[key] = self.n


class Prog:
    def __init__(self):
        self.emit = False
        self.engs = {}
        self.waited_keys = {n: set() for n in ("T", "V", "S", "G")}
        self.signals = {n: {} for n in ("T", "V", "S", "G")}
        self.dma = {}

    def eng(self, name):
        if name not in self.engs:
            self.engs[name] = Eng(self, name)
        return self.engs[name]

    def wait_mark(self, waiter_raw, producer, key):
        if not self.emit:
            self.waited_keys[producer.name].add(key)
            return
        idx = producer.marks[key]
        waiter_raw.wait_ge(producer.sem, self.signals[producer.name][idx])

    def finalize_pass1(self):
        for name, keys in self.waited_keys.items():
            marks = self.engs[name].marks
            idxs = sorted({marks[k] for k in keys})
            self.signals[name] = {i: r + 1 for r, i in enumerate(idxs)}
        for e in self.engs.values():
            e.n = 0
        self.dma = {}
        self.emit = True


def _newton_rsqrt(V, nc, out, vin, nw, add_c, mul_c, q, wide, tmp, iters=4, mark=None):
    if mul_c != 1.0:
        V.ins(lambda: nc.vector.tensor_scalar(q[:, :], vin, mul_c, add_c,
                                              op0=OP.mult, op1=OP.add))
    else:
        V.ins(lambda: nc.vector.tensor_scalar(q[:, :], vin, add_c, None, op0=OP.add))
    qi, oi = q[:, :].bitcast(I32), wide[:, :].bitcast(I32)
    V.ins(lambda: nc.vector.tensor_scalar(oi, qi, 1, None, op0=OP.arith_shift_right))
    V.ins(lambda: nc.vector.tensor_scalar(oi, oi, -1, None, op0=OP.bitwise_xor))
    V.ins(lambda: nc.vector.tensor_scalar(oi, oi, 0x5f3759df + 1, None, op0=OP.add))
    for _ in range(iters):
        V.ins(lambda: nc.vector.tensor_tensor(tmp[:, :], wide[:, :], wide[:, :], op=OP.mult))
        V.ins(lambda: nc.vector.tensor_tensor(tmp[:, :], tmp[:, :], q[:, :], op=OP.mult))
        V.ins(lambda: nc.vector.tensor_scalar(tmp[:, :], tmp[:, :], -0.5, 1.5,
                                              op0=OP.mult, op1=OP.add))
        V.ins(lambda: nc.vector.tensor_tensor(wide[:, :], wide[:, :], tmp[:, :], op=OP.mult))
    V.ins(lambda: nc.vector.tensor_copy(out, wide[:, 0:nw]), mark=mark)


def build(T):
    NS = T // SIG
    nc = bass.Bass("TRN2", target_bir_lowering=False, detect_race_conditions=False)

    xd = nc.dram_tensor("x", [T, D], F32, kind="ExternalInput").ap()
    win_d = nc.dram_tensor("w_in", [D, 3 * D], F32, kind="ExternalInput").ap()
    wout_d = nc.dram_tensor("w_out", [D, D], F32, kind="ExternalInput").ap()
    wr_d = nc.dram_tensor("w_r", [D, D], F32, kind="ExternalInput").ap()
    wk_d = nc.dram_tensor("w_k", [D, D], F32, kind="ExternalInput").ap()
    wv_d = nc.dram_tensor("w_v", [D, D], F32, kind="ExternalInput").ap()
    wn_d = nc.dram_tensor("wn", [D], F32, kind="ExternalInput").ap()
    ll_d = nc.dram_tensor("ll", [D], F32, kind="ExternalInput").ap()
    mur_d = nc.dram_tensor("mur", [D], F32, kind="ExternalInput").ap()
    muk_d = nc.dram_tensor("muk", [D], F32, kind="ExternalInput").ap()
    out_d = nc.dram_tensor("out", [T, D], F32, kind="ExternalOutput").ap()
    wr_s = nc.dram_tensor("wr_s", [D, D], BF16, kind="Internal").ap()
    wk_s = nc.dram_tensor("wk_s", [D, D], BF16, kind="Internal").ap()
    wv_s = nc.dram_tensor("wv_s", [D, D], BF16, kind="Internal").ap()

    with ExitStack() as ctx:
        def sb(name, shape, dtype):
            return ctx.enter_context(nc.sbuf_tensor(name, shape, dtype))

        def psum_(name, shape):
            return ctx.enter_context(nc.psum_tensor(name, shape, F32))

        reg = sb("reg", [128, NSLOT, D], F32)
        w_in = sb("w_in_sb", [128, KC, 3 * D], BF16)
        w_out = sb("w_out_sb", [128, KC, D], BF16)
        wrt = sb("wrt", [128, KC, D], BF16)
        wkt = sb("wkt", [128, KC, D], BF16)
        wvt = sb("wvt", [128, KC, D], BF16)
        xnf = sb("xnf", [128, KC, SIG], BF16)
        xn2f = sb("xn2f", [128, 2, KC, SIG], BF16)
        stok = sb("stok", [128, D], BF16)
        a_t = sb("a_t", [128, KC, SIG], F32)
        rgate = sb("rgate", [128, KC, SIG], F32)
        egel = sb("egel", [128, 2, SIG], F32)
        ub = sb("ub", [128, SIG], F32)
        ytmp = sb("ytmp", [128, 2, SIG], F32)
        ybf = sb("ybf", [128, KC, SIG], BF16)
        ycar = sb("ycar", [128, KC], F32)
        xrf = sb("xrf", [128, KC, SIG], BF16)
        xkf = sb("xkf", [128, KC, SIG], BF16)
        xdb = sb("xdb", [128, 2, SIG], BF16)
        e1b = sb("e1b", [128, 2, SIG], F32)
        kact = sb("kact", [128, KC, SIG], BF16)
        rsig = sb("rsig", [128, 3, 512], F32)
        vecs = sb("vecs", [128, 40], F32)
        stat = sb("stat", [128, NTT * 12], F32)
        mv = sb("mv", [128, NTT, 2], F32)
        vvv = sb("vvv", [128, NWID], F32)
        msc = vvv[:, 8:12]
        sc1 = vvv[:, 12:12 + NTT]
        sc2 = vvv[:, 16:16 + NTT]
        sc3 = vvv[:, 20:20 + NTT]
        scq = sb("scq", [128, NWID], F32)
        scw = sb("scw", [128, NWID], F32)
        sct = sb("sct", [128, NWID], F32)

        def vc(base, j):
            return vecs[:, base + j:base + j + 1]
        B_WN, B_LAMH, B_NLAMH, B_MUR1, B_MUK1 = 0, 8, 16, 24, 32

        pa = [psum_("pa0", [128, SIG]), psum_("pa1", [128, SIG])]
        po = [psum_("po0", [128, 512]), psum_("po1", [128, 512]), psum_("po2", [128, 512])]
        prv = [psum_("prv0", [128, 512]), psum_("prv1", [128, 512]), psum_("prv2", [128, 512])]

        s_ldx = ctx.enter_context(nc.semaphore("s_ldx"))
        s_dmat = ctx.enter_context(nc.semaphore("s_dmat"))
        s_st = ctx.enter_context(nc.semaphore("s_st"))
        s_ldg = ctx.enter_context(nc.semaphore("s_ldg"))
        sem_T = ctx.enter_context(nc.semaphore("sem_T"))
        sem_G = ctx.enter_context(nc.semaphore("sem_G"))
        sem_V = ctx.enter_context(nc.semaphore("sem_V"))
        sem_S = ctx.enter_context(nc.semaphore("sem_S"))

        P = Prog()
        TE, V, S, G = P.eng("T"), P.eng("V"), P.eng("S"), P.eng("G")

        pa_bank = {}
        po_bank = {}
        prv_bank = {}
        npa = npo = nprv = 0
        for s in range(NS + 3):
            sA, sW, sF = s - 1, s - 2, s - 3
            if 0 <= sW < NS:
                for H in range(4):
                    po_bank[(s, H)] = npo % 3
                    npo += 1
            if 0 <= sA < NS:
                for m in range(24):
                    pa_bank[(s, m)] = npa % 2
                    npa += 1
            if 0 <= sF < NS:
                for j in range(KC):
                    pa_bank[(s, 24 + j)] = npa % 2
                    npa += 1
                for H in range(4):
                    prv_bank[(s, "r", H)] = nprv % 3
                    nprv += 1
                for H in range(4):
                    prv_bank[(s, "v", H)] = nprv % 3
                    nprv += 1

        def emit_program():
            em = P.emit
            c = P.dma
            for k in ("ldx", "dmat", "st", "ldg"):
                c[k] = 0
            pa_free = []
            po_free = []
            prv_free = []
            xn_dmat_at = {}
            wt_dmat = [0]

            def dma(eng_raw, sem, key, out_ap_f, in_ap_f, transpose=False):
                c[key] += 16
                if em:
                    eng_raw.dma_start(out_ap_f(), in_ap_f(), transpose=transpose).then_inc(sem, 16)

            def dwait(eng_raw, sem, val):
                if em and val > 0:
                    eng_raw.wait_ge(sem, val)

            def gate(eng_raw, lst, nbank):
                g = len(lst)
                if g >= nbank:
                    who, key = lst[g - nbank]
                    P.wait_mark(eng_raw, who, key)

            if em:
                for dvec, base in [(wn_d, B_WN), (ll_d, B_LAMH), (mur_d, B_MUR1), (muk_d, B_MUK1)]:
                    nc.gpsimd.dma_start(vecs[:, base:base + 8],
                                        dvec.rearrange("(a b) -> b a", b=128),
                                        allow_slow_non_contiguous=True).then_inc(s_ldg, 16)
            c["ldg"] += 64
            for kc in range(KC):
                dma(nc.gpsimd, s_ldg, "ldg", lambda kc=kc: w_in[:, kc, :],
                    lambda kc=kc: win_d[kc * 128:(kc + 1) * 128, :])
            ldg_win = c["ldg"]
            for ec in range(KC):
                dma(nc.gpsimd, s_ldg, "ldg", lambda ec=ec: w_out[:, ec, :],
                    lambda ec=ec: wout_d[ec * 128:(ec + 1) * 128, :])
            ldg_wout = c["ldg"]
            dma(nc.gpsimd, s_ldg, "ldg", lambda: wr_s[:, :], lambda: wr_d[:, :])
            dma(nc.gpsimd, s_ldg, "ldg", lambda: wk_s[:, :], lambda: wk_d[:, :])
            dma(nc.gpsimd, s_ldg, "ldg", lambda: wv_s[:, :], lambda: wv_d[:, :])
            ldg_all = c["ldg"]

            dwait(nc.scalar, s_ldg, 64)
            S.ins(lambda: nc.scalar.activation(vecs[:, B_LAMH:B_LAMH + 8],
                                               vecs[:, B_LAMH:B_LAMH + 8], AF.Sigmoid), mark="lam")

            P.wait_mark(nc.vector, S, "lam")
            V.ins(lambda: nc.vector.tensor_scalar(vecs[:, B_NLAMH:B_NLAMH + 8],
                                                  vecs[:, B_LAMH:B_LAMH + 8], -1.0, None, op0=OP.mult))
            V.ins(lambda: nc.vector.tensor_scalar(vecs[:, B_MUR1:B_MUR1 + 8],
                                                  vecs[:, B_MUR1:B_MUR1 + 8], -1.0, None, op0=OP.add))
            V.ins(lambda: nc.vector.tensor_scalar(vecs[:, B_MUK1:B_MUK1 + 8],
                                                  vecs[:, B_MUK1:B_MUK1 + 8], -1.0, None, op0=OP.add))
            V.ins(lambda: nc.vector.memset(ycar[:, :], 0.0))
            V.ins(lambda: nc.vector.memset(vvv[:, :], 1.0))
            V.ins(lambda: nc.vector.memset(scq[:, :], 1.0))
            V.ins(lambda: nc.vector.memset(scw[:, :], 1.0))
            V.ins(lambda: nc.vector.memset(sct[:, :], 1.0))
            for j in range(KC - 1):
                V.ins(lambda j=j: nc.vector.memset(xn2f[:, 1, j, SIG - 1:SIG], 0.0))
            V.ins(lambda: nc.vector.memset(xn2f[:, 1, KC - 1, SIG - 1:SIG], 0.0), mark="prep")
            P.wait_mark(nc.tensor, V, ("fold", KC - 1))

            def ln_aggr_norm(tag, sg, scale_out, mul_c, add_c, extra_r1):
                for tt in range(NTT):
                    V.ins(lambda: nc.vector.memset(scq[:, 0:1], 0.0))
                    V.ins(lambda tt=tt: nc.vector.bn_aggr(mv[:, tt, :],
                                                          stat[:, tt * 12:tt * 12 + 12]),
                          mark=(tag + "ag", sg, tt))
                P.wait_mark(nc.vector, V, (tag + "ag", sg, NTT - 1))
                for tt in range(NTT):
                    V.ins(lambda tt=tt: nc.vector.tensor_copy(vvv[:, tt:tt + 1], mv[:, tt, 1:2]),
                          mark=(tag + "cp", sg, tt))
                P.wait_mark(nc.vector, V, (tag + "cp", sg, NTT - 1))
                if extra_r1:
                    _newton_rsqrt(V, nc, sc1, vvv[:, :], NTT, 1e-5, 1.0, scq, scw, sct)
                _newton_rsqrt(V, nc, scale_out[:, :], vvv[:, :], NTT, add_c, mul_c, scq, scw, sct,
                              mark=(tag + "nw", sg))
                P.wait_mark(nc.vector, V, (tag + "nw", sg))
                V.ins(lambda: nc.vector.tensor_tensor(msc[:, 0:NTT], mv[:, :, 0], scale_out[:, :],
                                                      op=OP.mult))
                if extra_r1:
                    V.ins(lambda: nc.vector.tensor_tensor(msc[:, NTT:2 * NTT], mv[:, :, 0],
                                                          sc1, op=OP.mult))
                    V.ins(lambda: nc.vector.tensor_scalar(sct[:, 0:NTT], sc1, 1.0, None,
                                                          op0=OP.add), mark=(tag + "ms", sg))
                else:
                    V.ins(lambda: nc.vector.tensor_copy(msc[:, 2:3], msc[:, 0:1]),
                          mark=(tag + "ms", sg))
                P.wait_mark(nc.vector, V, (tag + "ms", sg))

            for s in range(NS + 3):
                sL, sA, sW, sF = s, s - 1, s - 2, s - 3
                doL = sL < NS
                doA = 0 <= sA < NS
                doW = 0 <= sW < NS
                doF = 0 <= sF < NS

                if s == 0:
                    for tt in range(NTT):
                        gt = tt
                        dma(nc.sync, s_ldx, "ldx", lambda gt=gt: reg[:, gt % NSLOT, :],
                            lambda gt=gt: xd[gt * 128:(gt + 1) * 128, :])
                    dwait(nc.sync, s_ldx, c["ldx"])

                if doW:
                    if sW == 0:
                        dwait(nc.tensor, s_ldg, ldg_wout)
                    P.wait_mark(nc.tensor, V, ("y", sW))
                    for tc in range(NTT):
                        for half in range(2):
                            H = 2 * tc + half
                            gate(nc.tensor, po_free, 3)
                            po_free.append((V, ("pofree", s, H)))
                            b = po_bank[(s, H)]
                            for ec in range(KC):
                                TE.ins(lambda tc=tc, half=half, ec=ec, b=b: nc.tensor.matmul(
                                    po[b][:, :], ybf[:, ec, tc * 128:(tc + 1) * 128],
                                    w_out[:, ec, half * 512:(half + 1) * 512],
                                    start=(ec == 0), stop=(ec == KC - 1)),
                                    mark=(("po", s, H) if ec == KC - 1 else None))
                    TE.mark_here(("wout_done", s))

                if doA:
                    dwait(nc.tensor, s_dmat, xn_dmat_at[sA])
                    for m in range(24):
                        gate(nc.tensor, pa_free, 2)
                        pa_free.append((S, ("pafree", s, m)))
                        b = pa_bank[(s, m)]
                        for kc in range(KC):
                            TE.ins(lambda kc=kc, m=m, b=b: nc.tensor.matmul(
                                pa[b][:, :], w_in[:, kc, m * 128:(m + 1) * 128], xnf[:, kc, :],
                                start=(kc == 0), stop=(kc == KC - 1)),
                                mark=(("pa", s, m) if kc == KC - 1 else None))
                    TE.mark_here(("proj_done", sA))

                if doF:
                    if sF == 0:
                        dwait(nc.tensor, s_dmat, wt_dmat[0])
                    P.wait_mark(nc.tensor, V, ("mix", sF))
                    for j in range(KC):
                        gate(nc.tensor, pa_free, 2)
                        pa_free.append((S, ("pafree", s, 24 + j)))
                        b = pa_bank[(s, 24 + j)]
                        for cc in range(KC):
                            TE.ins(lambda cc=cc, j=j, b=b: nc.tensor.matmul(
                                pa[b][:, :], wkt[:, cc, j * 128:(j + 1) * 128], xkf[:, cc, :],
                                start=(cc == 0), stop=(cc == KC - 1)),
                                mark=(("pa", s, 24 + j) if cc == KC - 1 else None))
                    TE.mark_here(("k_done", s))
                    for tc in range(NTT):
                        for half in range(2):
                            H = 2 * tc + half
                            gate(nc.tensor, prv_free, 3)
                            prv_free.append((S, ("prvfree_r", s, H)))
                            b = prv_bank[(s, "r", H)]
                            for cc in range(KC):
                                TE.ins(lambda cc=cc, tc=tc, half=half, b=b: nc.tensor.matmul(
                                    prv[b][:, :], xrf[:, cc, tc * 128:(tc + 1) * 128],
                                    wrt[:, cc, half * 512:(half + 1) * 512],
                                    start=(cc == 0), stop=(cc == KC - 1)),
                                    mark=(("prv_r", s, H) if cc == KC - 1 else None))
                    TE.mark_here(("r_done", s))
                    P.wait_mark(nc.tensor, S, ("kact", sF, KC - 1))
                    for tc in range(NTT):
                        for half in range(2):
                            H = 2 * tc + half
                            gate(nc.tensor, prv_free, 3)
                            prv_free.append((V, ("prvfree_v", s, H)))
                            b = prv_bank[(s, "v", H)]
                            for dc in range(KC):
                                TE.ins(lambda dc=dc, tc=tc, half=half, b=b: nc.tensor.matmul(
                                    prv[b][:, :], kact[:, dc, tc * 128:(tc + 1) * 128],
                                    wvt[:, dc, half * 512:(half + 1) * 512],
                                    start=(dc == 0), stop=(dc == KC - 1)),
                                    mark=(("prv_v", s, H) if dc == KC - 1 else None))
                    TE.mark_here(("v_done", s))

                if doW:
                    for tc in range(NTT):
                        slot = (sW * NTT + tc) % NSLOT
                        for half in range(2):
                            H = 2 * tc + half
                            P.wait_mark(nc.vector, TE, ("po", s, H))
                            b = po_bank[(s, H)]
                            V.ins(lambda slot=slot, half=half, b=b: nc.vector.tensor_tensor(
                                reg[:, slot, half * 512:(half + 1) * 512],
                                reg[:, slot, half * 512:(half + 1) * 512],
                                po[b][:, :], op=OP.add), mark=("pofree", s, H))
                        V.ins(lambda slot=slot, tc=tc: nc.vector.bn_stats(
                            stat[:, tc * 12:tc * 12 + 6], reg[:, slot, 0:512]))
                        V.ins(lambda slot=slot, tc=tc: nc.vector.bn_stats(
                            stat[:, tc * 12 + 6:tc * 12 + 12], reg[:, slot, 512:1024]))
                    ln_aggr_norm("L2", sW, sc3, 1.0, 1e-5, extra_r1=False)
                    for tt in range(NTT):
                        slot = (sW * NTT + tt) % NSLOT
                        dwait(nc.vector, s_dmat, c["dmat"])
                        V.ins(lambda slot=slot, tt=tt: nc.vector.tensor_scalar(
                            stok[:, :], reg[:, slot, :], sc3[:, tt:tt + 1], msc[:, tt:tt + 1],
                            op0=OP.mult, op1=OP.subtract), mark=("ln2", sW, tt))

                        P.wait_mark(nc.sync, V, ("ln2", sW, tt))
                        if sW >= 1:
                            P.wait_mark(nc.sync, V, ("mix", sW - 1))
                        dma(nc.sync, s_dmat, "dmat",
                            lambda tt=tt, par=sW % 2: xn2f[:, par, :, tt * 128:(tt + 1) * 128],
                            lambda: stok[:, :], transpose=True)
                        dwait(nc.sync, s_dmat, c["dmat"])

                if doA:
                    if sA >= 1:
                        P.wait_mark(nc.scalar, V, ("scan", sA - 1))
                    for j in range(KC):
                        P.wait_mark(nc.scalar, TE, ("pa", s, j))
                        b = pa_bank[(s, j)]
                        S.ins(lambda j=j, b=b: nc.scalar.activation(
                            a_t[:, j, :], pa[b][:, :], AF.Sigmoid), mark=("pafree", s, j))
                        S.mark_here(("a", sA, j))
                    if sA >= 1:
                        P.wait_mark(nc.scalar, V, ("u1", sA - 1, KC - 1))
                    for j in range(KC):
                        P.wait_mark(nc.scalar, TE, ("pa", s, 8 + j))
                        b = pa_bank[(s, 8 + j)]
                        S.ins(lambda j=j, b=b: nc.scalar.activation(
                            rgate[:, j, :], pa[b][:, :], AF.Sigmoid), mark=("pafree", s, 8 + j))
                        S.mark_here(("r", sA, j))
                    for j in range(KC):
                        J = KC * sA + j
                        P.wait_mark(nc.scalar, TE, ("pa", s, 16 + j))
                        if J >= 2:
                            sp, jp = divmod(J - 2, KC)
                            P.wait_mark(nc.scalar, V, ("vp", sp, jp))
                        b = pa_bank[(s, 16 + j)]
                        S.ins(lambda j=j, b=b: nc.scalar.activation(
                            egel[:, j % 2, :], pa[b][:, :], AF.Gelu), mark=("pafree", s, 16 + j))
                        S.mark_here(("e", sA, j))

                if doA:
                    for j in range(KC):
                        P.wait_mark(nc.vector, S, ("a", sA, j))
                        V.ins(lambda j=j: nc.vector.tensor_scalar(ub[:, :], a_t[:, j, :],
                                                                  vc(B_NLAMH, j), vc(B_LAMH, j),
                                                                  op0=OP.mult, op1=OP.add))
                        P.wait_mark(nc.vector, S, ("r", sA, j))
                        V.ins(lambda j=j: nc.vector.tensor_tensor(ub[:, :], ub[:, :],
                                                                  rgate[:, j, :], op=OP.mult),
                              mark=("u1", sA, j))
                        P.wait_mark(nc.vector, S, ("e", sA, j))
                        V.ins(lambda j=j: nc.vector.tensor_tensor(ub[:, :], ub[:, :],
                                                                  egel[:, j % 2, :], op=OP.mult),
                              mark=("vp", sA, j))
                        V.ins(lambda j=j: nc.vector.tensor_tensor_scan(
                            ytmp[:, j % 2, :], a_t[:, j, :], ub[:, :], ycar[:, j:j + 1],
                            op0=OP.mult, op1=OP.add), mark=("scanop", sA, j))
                        if j == 1 and doW:
                            P.wait_mark(nc.vector, TE, ("wout_done", s))
                        if j >= 1:
                            V.ins(lambda j=j: nc.vector.tensor_copy(
                                ycar[:, j - 1:j], ytmp[:, (j - 1) % 2, SIG - 1:SIG]))
                            V.ins(lambda j=j: nc.vector.tensor_copy(
                                ybf[:, j - 1, :], ytmp[:, (j - 1) % 2, :]))
                    P.wait_mark(nc.vector, V, ("scanop", sA, KC - 1))
                    V.ins(lambda: nc.vector.tensor_copy(ycar[:, KC - 1:KC],
                                                        ytmp[:, (KC - 1) % 2, SIG - 1:SIG]))
                    V.ins(lambda: nc.vector.tensor_copy(ybf[:, KC - 1, :], ytmp[:, (KC - 1) % 2, :]),
                          mark=("y", sA))
                    V.mark_here(("scan", sA))

                if doL:
                    dwait(nc.vector, s_ldx, c["ldx"])
                    for tt in range(NTT):
                        slot = (sL * NTT + tt) % NSLOT
                        V.ins(lambda slot=slot, tt=tt: nc.vector.bn_stats(
                            stat[:, tt * 12:tt * 12 + 6], reg[:, slot, 0:512]))
                        V.ins(lambda slot=slot, tt=tt: nc.vector.bn_stats(
                            stat[:, tt * 12 + 6:tt * 12 + 12], reg[:, slot, 512:1024]))
                    ln_aggr_norm("L1", sL, sc2, 1.000001, 1e-11, extra_r1=True)
                    for tt in range(NTT):
                        slot = (sL * NTT + tt) % NSLOT
                        dwait(nc.vector, s_dmat, c["dmat"])
                        V.ins(lambda slot=slot, tt=tt: nc.vector.tensor_scalar(
                            stok[:, :], reg[:, slot, :], sc2[:, tt:tt + 1], msc[:, tt:tt + 1],
                            op0=OP.mult, op1=OP.subtract), mark=("ln1", sL, tt))
                        V.ins(lambda slot=slot, tt=tt: nc.vector.tensor_scalar(
                            reg[:, slot, :], reg[:, slot, :], sct[:, tt:tt + 1],
                            msc[:, NTT + tt:NTT + tt + 1], op0=OP.mult, op1=OP.subtract))

                        P.wait_mark(nc.sync, V, ("ln1", sL, tt))
                        if sL >= 1:
                            P.wait_mark(nc.sync, TE, ("proj_done", sL - 1))
                        dma(nc.sync, s_dmat, "dmat",
                            lambda tt=tt: xnf[:, :, tt * 128:(tt + 1) * 128], lambda: stok[:, :],
                            transpose=True)
                        dwait(nc.sync, s_dmat, c["dmat"])
                    xn_dmat_at[sL] = c["dmat"]
                if s == 0:
                    for kc in range(KC):
                        dwait(nc.vector, s_ldg, 64 + 16 * (kc + 1))
                        V.ins(lambda kc=kc: nc.vector.tensor_scalar(
                            w_in[:, kc, :], w_in[:, kc, :], vc(B_WN, kc), None, op0=OP.mult),
                            mark=("fold", kc))
                if s == 1:
                    dwait(nc.sync, s_ldg, ldg_all)
                    for scr, dst in [(wr_s, wrt), (wk_s, wkt), (wv_s, wvt)]:
                        dma(nc.sync, s_dmat, "dmat", lambda dst=dst: dst[:, :, :],
                            lambda scr=scr: scr[:, :], transpose=True)
                        dwait(nc.sync, s_dmat, c["dmat"])
                    wt_dmat[0] = c["dmat"]

                if doF:
                    if sF >= 1:
                        P.wait_mark(nc.scalar, TE, ("v_done", s - 1))
                    for j in range(KC):
                        P.wait_mark(nc.scalar, TE, ("pa", s, 24 + j))
                        b = pa_bank[(s, 24 + j)]
                        S.ins(lambda j=j, b=b: nc.scalar.activation(
                            e1b[:, j % 2, :], pa[b][:, :], AF.Relu), mark=("pafree", s, 24 + j))
                        S.ins(lambda j=j: nc.scalar.activation(
                            kact[:, j, :], e1b[:, j % 2, :], AF.Square), mark=("kact", sF, j))
                    for H in range(4):
                        P.wait_mark(nc.scalar, TE, ("prv_r", s, H))
                        if 4 * sF + H >= 3:
                            hp, hh = divmod(4 * sF + H - 3, 4)
                            P.wait_mark(nc.scalar, V, ("rv", hp, hh))
                        b = prv_bank[(s, "r", H)]
                        S.ins(lambda H=H, b=b: nc.scalar.activation(
                            rsig[:, H % 3, :], prv[b][:, :], AF.Sigmoid), mark=("prvfree_r", s, H))
                        S.mark_here(("rs", sF, H))

                if doW:
                    dwait(nc.vector, s_dmat, c["dmat"])
                    if doF:
                        P.wait_mark(nc.vector, TE, ("k_done", s))
                        P.wait_mark(nc.vector, TE, ("r_done", s))
                    par = sW % 2
                    for j in range(KC):
                        xv = xn2f[:, par, j, :]
                        V.ins(lambda xv=xv, j=j: nc.vector.tensor_tensor(
                            xdb[:, j % 2, 1:SIG], xv[:, 1:SIG], xv[:, 0:SIG - 1], op=OP.subtract))
                        V.ins(lambda xv=xv, j=j, par=par: nc.vector.tensor_tensor(
                            xdb[:, j % 2, 0:1], xv[:, 0:1], xn2f[:, 1 - par, j, SIG - 1:SIG],
                            op=OP.subtract))
                        if j >= 1:
                            V.ins(lambda j=j, par=par: nc.vector.scalar_tensor_tensor(
                                xrf[:, j - 1, :], xdb[:, (j - 1) % 2, :], vc(B_MUR1, j - 1),
                                xn2f[:, par, j - 1, :], op0=OP.mult, op1=OP.add))
                            V.ins(lambda j=j, par=par: nc.vector.scalar_tensor_tensor(
                                xkf[:, j - 1, :], xdb[:, (j - 1) % 2, :], vc(B_MUK1, j - 1),
                                xn2f[:, par, j - 1, :], op0=OP.mult, op1=OP.add))
                    V.ins(lambda par=par: nc.vector.scalar_tensor_tensor(
                        xrf[:, KC - 1, :], xdb[:, (KC - 1) % 2, :], vc(B_MUR1, KC - 1),
                        xn2f[:, par, KC - 1, :], op0=OP.mult, op1=OP.add))
                    V.ins(lambda par=par: nc.vector.scalar_tensor_tensor(
                        xkf[:, KC - 1, :], xdb[:, (KC - 1) % 2, :], vc(B_MUK1, KC - 1),
                        xn2f[:, par, KC - 1, :], op0=OP.mult, op1=OP.add), mark=("mix", sW))

                if doF:
                    for H in range(4):
                        tc, half = divmod(H, 2)
                        slot = (sF * NTT + tc) % NSLOT
                        P.wait_mark(nc.vector, TE, ("prv_v", s, H))
                        P.wait_mark(nc.vector, S, ("rs", sF, H))
                        b = prv_bank[(s, "v", H)]
                        rvv = ytmp[:, :, :].rearrange("p a b -> p (a b)")
                        V.ins(lambda b=b, H=H, rvv=rvv: nc.vector.tensor_tensor(
                            rvv, prv[b][:, :], rsig[:, H % 3, :], op=OP.mult))
                        V.ins(lambda slot=slot, half=half, rvv=rvv: nc.vector.tensor_tensor(
                            reg[:, slot, half * 512:(half + 1) * 512],
                            reg[:, slot, half * 512:(half + 1) * 512], rvv,
                            op=OP.add), mark=("rv", sF, H))
                        V.mark_here(("prvfree_v", s, H))
                        if half == 1:
                            V.mark_here(("x2", sF, tc))

                if doF:
                    for tt in range(NTT):
                        gt = sF * NTT + tt
                        P.wait_mark(nc.sync, V, ("x2", sF, tt))
                        dma(nc.sync, s_st, "st", lambda gt=gt: out_d[gt * 128:(gt + 1) * 128, :],
                            lambda gt=gt: reg[:, gt % NSLOT, :])
                    dwait(nc.sync, s_st, c["st"])
                if s + 1 < NS:
                    for tt in range(NTT):
                        gt = (s + 1) * NTT + tt
                        dma(nc.sync, s_ldx, "ldx", lambda gt=gt: reg[:, gt % NSLOT, :],
                            lambda gt=gt: xd[gt * 128:(gt + 1) * 128, :])
                    dwait(nc.sync, s_ldx, c["ldx"])

            dwait(nc.sync, s_st, 16 * 2 * NS)

        emit_program()
        P.finalize_pass1()
        TE.sem, V.sem, S.sem, G.sem = sem_T, sem_V, sem_S, sem_G
        emit_program()
    return nc


def kernel(L0_attn_x, L0_attn_w_norm, L0_attn_W_in, L0_attn_lambda_log, L0_attn_W_out,
           L0_ffn_W_r, L0_ffn_W_k, L0_ffn_W_v, L0_ffn_mu_r, L0_ffn_mu_k):
    B, T, D_ = L0_attn_x.shape
    assert (B, T, D_) == (8, 2048, 1024)
    nc = build(T)
    w = {
        "w_in": np.ascontiguousarray(L0_attn_W_in, dtype=np.float32),
        "w_out": np.ascontiguousarray(L0_attn_W_out, dtype=np.float32),
        "w_r": np.ascontiguousarray(L0_ffn_W_r, dtype=np.float32),
        "w_k": np.ascontiguousarray(L0_ffn_W_k, dtype=np.float32),
        "w_v": np.ascontiguousarray(L0_ffn_W_v, dtype=np.float32),
        "wn": np.ascontiguousarray(L0_attn_w_norm, dtype=np.float32),
        "ll": np.ascontiguousarray(L0_attn_lambda_log, dtype=np.float32),
        "mur": np.ascontiguousarray(L0_ffn_mu_r, dtype=np.float32),
        "muk": np.ascontiguousarray(L0_ffn_mu_k, dtype=np.float32),
    }
    in_maps = [dict(w, x=np.ascontiguousarray(L0_attn_x[i], dtype=np.float32)) for i in range(B)]
    res = run_bass_kernel_spmd(nc, in_maps, core_ids=list(range(B)))
    return np.stack([res.results[i]["out"] for i in range(B)], axis=0)


# version 16
# speedup vs baseline: 1.5280x; 1.0533x over previous
import numpy as np
from contextlib import ExitStack

import concourse.bass as bass
import concourse.mybir as mybir
from concourse.alu_op_type import AluOpType as OP
from concourse.bass_utils import run_bass_kernel_spmd

F32 = mybir.dt.float32
BF16 = mybir.dt.bfloat16
I32 = mybir.dt.int32
AF = mybir.ActivationFunctionType

D = 1024
KC = 8
SIG = 256
NTT = SIG // 128
NSLOT = 8
NWID = 64


class Eng:
    def __init__(self, prog, name):
        self.prog = prog
        self.name = name
        self.n = 0
        self.marks = {}
        self.sem = None

    def ins(self, f, mark=None):
        self.n += 1
        if self.prog.emit:
            instr = f()
            if self.n in self.prog.signals[self.name]:
                instr.then_inc(self.sem, 1)
        if mark is not None:
            if self.prog.emit:
                assert self.marks.get(mark) == self.n, \
                    f"pass drift: {self.name} {mark}: {self.marks.get(mark)} vs {self.n}"
            self.marks[mark] = self.n
        return self.n

    def mark_here(self, key):
        if self.prog.emit:
            assert self.marks.get(key) == self.n
        self.marks[key] = self.n


class Prog:
    def __init__(self):
        self.emit = False
        self.engs = {}
        self.waited_keys = {n: set() for n in ("T", "V", "S", "G")}
        self.signals = {n: {} for n in ("T", "V", "S", "G")}
        self.dma = {}

    def eng(self, name):
        if name not in self.engs:
            self.engs[name] = Eng(self, name)
        return self.engs[name]

    def wait_mark(self, waiter_raw, producer, key):
        if not self.emit:
            self.waited_keys[producer.name].add(key)
            return
        idx = producer.marks[key]
        waiter_raw.wait_ge(producer.sem, self.signals[producer.name][idx])

    def finalize_pass1(self):
        for name, keys in self.waited_keys.items():
            marks = self.engs[name].marks
            idxs = sorted({marks[k] for k in keys})
            self.signals[name] = {i: r + 1 for r, i in enumerate(idxs)}
        for e in self.engs.values():
            e.n = 0
        self.dma = {}
        self.emit = True


def _newton_rsqrt(V, nc, out, vin, nw, add_c, mul_c, q, wide, tmp, iters=4, mark=None):
    if mul_c != 1.0:
        V.ins(lambda: nc.vector.tensor_scalar(q[:, :], vin, mul_c, add_c,
                                              op0=OP.mult, op1=OP.add))
    else:
        V.ins(lambda: nc.vector.tensor_scalar(q[:, :], vin, add_c, None, op0=OP.add))
    qi, oi = q[:, :].bitcast(I32), wide[:, :].bitcast(I32)
    V.ins(lambda: nc.vector.tensor_scalar(oi, qi, 1, None, op0=OP.arith_shift_right))
    V.ins(lambda: nc.vector.tensor_scalar(oi, oi, -1, None, op0=OP.bitwise_xor))
    V.ins(lambda: nc.vector.tensor_scalar(oi, oi, 0x5f3759df + 1, None, op0=OP.add))
    for _ in range(iters):
        V.ins(lambda: nc.vector.tensor_tensor(tmp[:, :], wide[:, :], wide[:, :], op=OP.mult))
        V.ins(lambda: nc.vector.tensor_tensor(tmp[:, :], tmp[:, :], q[:, :], op=OP.mult))
        V.ins(lambda: nc.vector.tensor_scalar(tmp[:, :], tmp[:, :], -0.5, 1.5,
                                              op0=OP.mult, op1=OP.add))
        V.ins(lambda: nc.vector.tensor_tensor(wide[:, :], wide[:, :], tmp[:, :], op=OP.mult))
    V.ins(lambda: nc.vector.tensor_copy(out, wide[:, 0:nw]), mark=mark)


def build(T):
    NS = T // SIG
    nc = bass.Bass("TRN2", target_bir_lowering=False, detect_race_conditions=False)

    xd = nc.dram_tensor("x", [T, D], F32, kind="ExternalInput").ap()
    win_d = nc.dram_tensor("w_in", [D, 3 * D], F32, kind="ExternalInput").ap()
    wout_d = nc.dram_tensor("w_out", [D, D], F32, kind="ExternalInput").ap()
    wr_d = nc.dram_tensor("w_r", [D, D], F32, kind="ExternalInput").ap()
    wk_d = nc.dram_tensor("w_k", [D, D], F32, kind="ExternalInput").ap()
    wv_d = nc.dram_tensor("w_v", [D, D], F32, kind="ExternalInput").ap()
    wn_d = nc.dram_tensor("wn", [D], F32, kind="ExternalInput").ap()
    ll_d = nc.dram_tensor("ll", [D], F32, kind="ExternalInput").ap()
    mur_d = nc.dram_tensor("mur", [D], F32, kind="ExternalInput").ap()
    muk_d = nc.dram_tensor("muk", [D], F32, kind="ExternalInput").ap()
    out_d = nc.dram_tensor("out", [T, D], F32, kind="ExternalOutput").ap()
    wr_s = nc.dram_tensor("wr_s", [D, D], BF16, kind="Internal").ap()
    wk_s = nc.dram_tensor("wk_s", [D, D], BF16, kind="Internal").ap()
    wv_s = nc.dram_tensor("wv_s", [D, D], BF16, kind="Internal").ap()

    with ExitStack() as ctx:
        def sb(name, shape, dtype):
            return ctx.enter_context(nc.sbuf_tensor(name, shape, dtype))

        def psum_(name, shape):
            return ctx.enter_context(nc.psum_tensor(name, shape, F32))

        reg = sb("reg", [128, NSLOT, D], F32)
        w_in = sb("w_in_sb", [128, KC, 3 * D], BF16)
        w_out = sb("w_out_sb", [128, KC, D], BF16)
        wrt = sb("wrt", [128, KC, D], BF16)
        wkt = sb("wkt", [128, KC, D], BF16)
        wvt = sb("wvt", [128, KC, D], BF16)
        xnf = sb("xnf", [128, KC, SIG], BF16)
        xn2f = sb("xn2f", [128, 2, KC, SIG], BF16)
        stok = sb("stok", [128, D], BF16)
        a_t = sb("a_t", [128, KC, SIG], F32)
        rgate = sb("rgate", [128, KC, SIG], F32)
        egel = sb("egel", [128, 4, SIG], F32)
        ub = sb("ub", [128, SIG], F32)
        ytmp = sb("ytmp", [128, 2, SIG], F32)
        ybf = sb("ybf", [128, KC, SIG], BF16)
        ycar = sb("ycar", [128, KC], F32)
        xrf = sb("xrf", [128, KC, SIG], BF16)
        xkf = sb("xkf", [128, KC, SIG], BF16)
        xdb = sb("xdb", [128, 2, SIG], BF16)
        e1b = sb("e1b", [128, 2, SIG], F32)
        kact = sb("kact", [128, KC, SIG], BF16)
        rsig = sb("rsig", [128, 3, 512], F32)
        vecs = sb("vecs", [128, 40], F32)
        stat = sb("stat", [128, NTT * 12], F32)
        mv = sb("mv", [128, NTT, 2], F32)
        vvv = sb("vvv", [128, NWID], F32)
        msc = vvv[:, 8:12]
        sc1 = vvv[:, 12:12 + NTT]
        sc2 = vvv[:, 16:16 + NTT]
        sc3 = vvv[:, 20:20 + NTT]
        scq = sb("scq", [128, NWID], F32)
        scw = sb("scw", [128, NWID], F32)
        sct = sb("sct", [128, NWID], F32)

        def vc(base, j):
            return vecs[:, base + j:base + j + 1]
        B_WN, B_LAMH, B_NLAMH, B_MUR1, B_MUK1 = 0, 8, 16, 24, 32

        pa = [psum_("pa0", [128, SIG]), psum_("pa1", [128, SIG])]
        po = [psum_("po0", [128, 512]), psum_("po1", [128, 512]), psum_("po2", [128, 512])]
        prv = [psum_("prv0", [128, 512]), psum_("prv1", [128, 512]), psum_("prv2", [128, 512])]

        s_ldx = ctx.enter_context(nc.semaphore("s_ldx"))
        s_dmat = ctx.enter_context(nc.semaphore("s_dmat"))
        s_st = ctx.enter_context(nc.semaphore("s_st"))
        s_ldg = ctx.enter_context(nc.semaphore("s_ldg"))
        sem_T = ctx.enter_context(nc.semaphore("sem_T"))
        sem_G = ctx.enter_context(nc.semaphore("sem_G"))
        sem_V = ctx.enter_context(nc.semaphore("sem_V"))
        sem_S = ctx.enter_context(nc.semaphore("sem_S"))

        P = Prog()
        TE, V, S, G = P.eng("T"), P.eng("V"), P.eng("S"), P.eng("G")

        pa_bank = {}
        po_bank = {}
        prv_bank = {}
        npa = npo = nprv = 0
        for s in range(NS + 3):
            sA, sW, sF = s - 1, s - 2, s - 3
            if 0 <= sW < NS:
                for H in range(4):
                    po_bank[(s, H)] = npo % 3
                    npo += 1
            if 0 <= sA < NS:
                for m in range(24):
                    pa_bank[(s, m)] = npa % 2
                    npa += 1
            if 0 <= sF < NS:
                for j in range(KC):
                    pa_bank[(s, 24 + j)] = npa % 2
                    npa += 1
                for H in range(4):
                    prv_bank[(s, "r", H)] = nprv % 3
                    nprv += 1
                for H in range(4):
                    prv_bank[(s, "v", H)] = nprv % 3
                    nprv += 1

        def emit_program():
            em = P.emit
            c = P.dma
            for k in ("ldx", "dmat", "st", "ldg"):
                c[k] = 0
            pa_free = []
            po_free = []
            prv_free = []
            xn_dmat_at = {}
            wt_dmat = [0]

            def dma(eng_raw, sem, key, out_ap_f, in_ap_f, transpose=False):
                c[key] += 16
                if em:
                    eng_raw.dma_start(out_ap_f(), in_ap_f(), transpose=transpose).then_inc(sem, 16)

            def dwait(eng_raw, sem, val):
                if em and val > 0:
                    eng_raw.wait_ge(sem, val)

            def gate(eng_raw, lst, nbank):
                g = len(lst)
                if g >= nbank:
                    who, key = lst[g - nbank]
                    P.wait_mark(eng_raw, who, key)

            if em:
                for dvec, base in [(wn_d, B_WN), (ll_d, B_LAMH), (mur_d, B_MUR1), (muk_d, B_MUK1)]:
                    nc.gpsimd.dma_start(vecs[:, base:base + 8],
                                        dvec.rearrange("(a b) -> b a", b=128),
                                        allow_slow_non_contiguous=True).then_inc(s_ldg, 16)
            c["ldg"] += 64
            for kc in range(KC):
                dma(nc.gpsimd, s_ldg, "ldg", lambda kc=kc: w_in[:, kc, :],
                    lambda kc=kc: win_d[kc * 128:(kc + 1) * 128, :])
            ldg_win = c["ldg"]
            dwait(nc.gpsimd, s_dmat, 32)
            for ec in range(KC):
                dma(nc.gpsimd, s_ldg, "ldg", lambda ec=ec: w_out[:, ec, :],
                    lambda ec=ec: wout_d[ec * 128:(ec + 1) * 128, :])
            ldg_wout = c["ldg"]
            dma(nc.gpsimd, s_ldg, "ldg", lambda: wr_s[:, :], lambda: wr_d[:, :])
            dma(nc.gpsimd, s_ldg, "ldg", lambda: wk_s[:, :], lambda: wk_d[:, :])
            dma(nc.gpsimd, s_ldg, "ldg", lambda: wv_s[:, :], lambda: wv_d[:, :])
            ldg_all = c["ldg"]

            dwait(nc.scalar, s_ldg, 64)
            S.ins(lambda: nc.scalar.activation(vecs[:, B_LAMH:B_LAMH + 8],
                                               vecs[:, B_LAMH:B_LAMH + 8], AF.Sigmoid), mark="lam")

            P.wait_mark(nc.vector, S, "lam")
            V.ins(lambda: nc.vector.tensor_scalar(vecs[:, B_NLAMH:B_NLAMH + 8],
                                                  vecs[:, B_LAMH:B_LAMH + 8], -1.0, None, op0=OP.mult))
            V.ins(lambda: nc.vector.tensor_scalar(vecs[:, B_MUR1:B_MUR1 + 8],
                                                  vecs[:, B_MUR1:B_MUR1 + 8], -1.0, None, op0=OP.add))
            V.ins(lambda: nc.vector.tensor_scalar(vecs[:, B_MUK1:B_MUK1 + 8],
                                                  vecs[:, B_MUK1:B_MUK1 + 8], -1.0, None, op0=OP.add))
            V.ins(lambda: nc.vector.memset(ycar[:, :], 0.0))
            V.ins(lambda: nc.vector.memset(vvv[:, :], 1.0))
            V.ins(lambda: nc.vector.memset(scq[:, :], 1.0))
            V.ins(lambda: nc.vector.memset(scw[:, :], 1.0))
            V.ins(lambda: nc.vector.memset(sct[:, :], 1.0))
            for j in range(KC - 1):
                V.ins(lambda j=j: nc.vector.memset(xn2f[:, 1, j, SIG - 1:SIG], 0.0))
            V.ins(lambda: nc.vector.memset(xn2f[:, 1, KC - 1, SIG - 1:SIG], 0.0), mark="prep")
            P.wait_mark(nc.tensor, V, ("fold", KC - 1))

            def ln_aggr_norm(tag, sg, scale_out, mul_c, add_c, extra_r1):
                for tt in range(NTT):
                    V.ins(lambda: nc.vector.memset(scq[:, 0:1], 0.0))
                    V.ins(lambda tt=tt: nc.vector.bn_aggr(mv[:, tt, :],
                                                          stat[:, tt * 12:tt * 12 + 12]),
                          mark=(tag + "ag", sg, tt))
                P.wait_mark(nc.vector, V, (tag + "ag", sg, NTT - 1))
                for tt in range(NTT):
                    V.ins(lambda tt=tt: nc.vector.tensor_copy(vvv[:, tt:tt + 1], mv[:, tt, 1:2]),
                          mark=(tag + "cp", sg, tt))
                P.wait_mark(nc.vector, V, (tag + "cp", sg, NTT - 1))
                if extra_r1:
                    _newton_rsqrt(V, nc, sc1, vvv[:, :], NTT, 1e-5, 1.0, scq, scw, sct)
                _newton_rsqrt(V, nc, scale_out[:, :], vvv[:, :], NTT, add_c, mul_c, scq, scw, sct,
                              mark=(tag + "nw", sg))
                P.wait_mark(nc.vector, V, (tag + "nw", sg))
                V.ins(lambda: nc.vector.tensor_tensor(msc[:, 0:NTT], mv[:, :, 0], scale_out[:, :],
                                                      op=OP.mult))
                if extra_r1:
                    V.ins(lambda: nc.vector.tensor_tensor(msc[:, NTT:2 * NTT], mv[:, :, 0],
                                                          sc1, op=OP.mult))
                    V.ins(lambda: nc.vector.tensor_scalar(sct[:, 0:NTT], sc1, 1.0, None,
                                                          op0=OP.add), mark=(tag + "ms", sg))
                else:
                    V.ins(lambda: nc.vector.tensor_copy(msc[:, 2:3], msc[:, 0:1]),
                          mark=(tag + "ms", sg))
                P.wait_mark(nc.vector, V, (tag + "ms", sg))

            for s in range(NS + 3):
                sL, sA, sW, sF = s, s - 1, s - 2, s - 3
                doL = sL < NS
                doA = 0 <= sA < NS
                doW = 0 <= sW < NS
                doF = 0 <= sF < NS

                if s == 0:
                    for tt in range(NTT):
                        gt = tt
                        dma(nc.sync, s_ldx, "ldx", lambda gt=gt: reg[:, gt % NSLOT, :],
                            lambda gt=gt: xd[gt * 128:(gt + 1) * 128, :])
                    dwait(nc.sync, s_ldx, c["ldx"])

                if doW:
                    if sW == 0:
                        dwait(nc.tensor, s_ldg, ldg_wout)
                    P.wait_mark(nc.tensor, V, ("y", sW))
                    for tc in range(NTT):
                        for half in range(2):
                            H = 2 * tc + half
                            gate(nc.tensor, po_free, 3)
                            po_free.append((V, ("pofree", s, H)))
                            b = po_bank[(s, H)]
                            for ec in range(KC):
                                TE.ins(lambda tc=tc, half=half, ec=ec, b=b: nc.tensor.matmul(
                                    po[b][:, :], ybf[:, ec, tc * 128:(tc + 1) * 128],
                                    w_out[:, ec, half * 512:(half + 1) * 512],
                                    start=(ec == 0), stop=(ec == KC - 1)),
                                    mark=(("po", s, H) if ec == KC - 1 else None))
                    TE.mark_here(("wout_done", s))

                if doA:
                    dwait(nc.tensor, s_dmat, xn_dmat_at[sA])
                    for m in range(24):
                        gate(nc.tensor, pa_free, 2)
                        pa_free.append((S, ("pafree", s, m)))
                        b = pa_bank[(s, m)]
                        for kc in range(KC):
                            TE.ins(lambda kc=kc, m=m, b=b: nc.tensor.matmul(
                                pa[b][:, :], w_in[:, kc, m * 128:(m + 1) * 128], xnf[:, kc, :],
                                start=(kc == 0), stop=(kc == KC - 1)),
                                mark=(("pa", s, m) if kc == KC - 1 else None))
                    TE.mark_here(("proj_done", sA))

                if doF:
                    if sF == 0:
                        dwait(nc.tensor, s_dmat, wt_dmat[0])
                    P.wait_mark(nc.tensor, V, ("mix", sF))
                    for j in range(KC):
                        gate(nc.tensor, pa_free, 2)
                        pa_free.append((S, ("pafree", s, 24 + j)))
                        b = pa_bank[(s, 24 + j)]
                        for cc in range(KC):
                            TE.ins(lambda cc=cc, j=j, b=b: nc.tensor.matmul(
                                pa[b][:, :], wkt[:, cc, j * 128:(j + 1) * 128], xkf[:, cc, :],
                                start=(cc == 0), stop=(cc == KC - 1)),
                                mark=(("pa", s, 24 + j) if cc == KC - 1 else None))
                    TE.mark_here(("k_done", s))
                    for tc in range(NTT):
                        for half in range(2):
                            H = 2 * tc + half
                            gate(nc.tensor, prv_free, 3)
                            prv_free.append((S, ("prvfree_r", s, H)))
                            b = prv_bank[(s, "r", H)]
                            for cc in range(KC):
                                TE.ins(lambda cc=cc, tc=tc, half=half, b=b: nc.tensor.matmul(
                                    prv[b][:, :], xrf[:, cc, tc * 128:(tc + 1) * 128],
                                    wrt[:, cc, half * 512:(half + 1) * 512],
                                    start=(cc == 0), stop=(cc == KC - 1)),
                                    mark=(("prv_r", s, H) if cc == KC - 1 else None))
                    TE.mark_here(("r_done", s))
                    P.wait_mark(nc.tensor, S, ("kact", sF, KC - 1))
                    for tc in range(NTT):
                        for half in range(2):
                            H = 2 * tc + half
                            gate(nc.tensor, prv_free, 3)
                            prv_free.append((V, ("prvfree_v", s, H)))
                            b = prv_bank[(s, "v", H)]
                            for dc in range(KC):
                                TE.ins(lambda dc=dc, tc=tc, half=half, b=b: nc.tensor.matmul(
                                    prv[b][:, :], kact[:, dc, tc * 128:(tc + 1) * 128],
                                    wvt[:, dc, half * 512:(half + 1) * 512],
                                    start=(dc == 0), stop=(dc == KC - 1)),
                                    mark=(("prv_v", s, H) if dc == KC - 1 else None))
                    TE.mark_here(("v_done", s))

                if doW:
                    for tc in range(NTT):
                        slot = (sW * NTT + tc) % NSLOT
                        for half in range(2):
                            H = 2 * tc + half
                            P.wait_mark(nc.vector, TE, ("po", s, H))
                            b = po_bank[(s, H)]
                            V.ins(lambda slot=slot, half=half, b=b: nc.vector.tensor_tensor(
                                reg[:, slot, half * 512:(half + 1) * 512],
                                reg[:, slot, half * 512:(half + 1) * 512],
                                po[b][:, :], op=OP.add), mark=("pofree", s, H))
                        V.ins(lambda slot=slot, tc=tc: nc.vector.bn_stats(
                            stat[:, tc * 12:tc * 12 + 6], reg[:, slot, 0:512]))
                        V.ins(lambda slot=slot, tc=tc: nc.vector.bn_stats(
                            stat[:, tc * 12 + 6:tc * 12 + 12], reg[:, slot, 512:1024]))
                    ln_aggr_norm("L2", sW, sc3, 1.0, 1e-5, extra_r1=False)
                    for tt in range(NTT):
                        slot = (sW * NTT + tt) % NSLOT
                        dwait(nc.vector, s_dmat, c["dmat"])
                        V.ins(lambda slot=slot, tt=tt: nc.vector.tensor_scalar(
                            stok[:, :], reg[:, slot, :], sc3[:, tt:tt + 1], msc[:, tt:tt + 1],
                            op0=OP.mult, op1=OP.subtract), mark=("ln2", sW, tt))

                        P.wait_mark(nc.sync, V, ("ln2", sW, tt))
                        if sW >= 1:
                            P.wait_mark(nc.sync, V, ("mix", sW - 1))
                        dma(nc.sync, s_dmat, "dmat",
                            lambda tt=tt, par=sW % 2: xn2f[:, par, :, tt * 128:(tt + 1) * 128],
                            lambda: stok[:, :], transpose=True)
                        dwait(nc.sync, s_dmat, c["dmat"])

                if doA:
                    if sA >= 1:
                        P.wait_mark(nc.scalar, V, ("scan", sA - 1))
                    for j in range(KC):
                        P.wait_mark(nc.scalar, TE, ("pa", s, j))
                        b = pa_bank[(s, j)]
                        S.ins(lambda j=j, b=b: nc.scalar.activation(
                            a_t[:, j, :], pa[b][:, :], AF.Sigmoid), mark=("pafree", s, j))
                        S.mark_here(("a", sA, j))
                    if sA >= 1:
                        P.wait_mark(nc.scalar, V, ("u1", sA - 1, KC - 1))
                    for j in range(KC):
                        P.wait_mark(nc.scalar, TE, ("pa", s, 8 + j))
                        b = pa_bank[(s, 8 + j)]
                        S.ins(lambda j=j, b=b: nc.scalar.activation(
                            rgate[:, j, :], pa[b][:, :], AF.Sigmoid), mark=("pafree", s, 8 + j))
                        S.mark_here(("r", sA, j))
                    for j in range(KC):
                        J = KC * sA + j
                        P.wait_mark(nc.scalar, TE, ("pa", s, 16 + j))
                        if J >= 4:
                            sp, jp = divmod(J - 4, KC)
                            P.wait_mark(nc.scalar, V, ("vp", sp, jp))
                        b = pa_bank[(s, 16 + j)]
                        S.ins(lambda j=j, b=b: nc.scalar.activation(
                            egel[:, j % 4, :], pa[b][:, :], AF.Gelu), mark=("pafree", s, 16 + j))
                        S.mark_here(("e", sA, j))

                if doA:
                    for j in range(KC):
                        P.wait_mark(nc.vector, S, ("a", sA, j))
                        V.ins(lambda j=j: nc.vector.tensor_scalar(ub[:, :], a_t[:, j, :],
                                                                  vc(B_NLAMH, j), vc(B_LAMH, j),
                                                                  op0=OP.mult, op1=OP.add))
                        P.wait_mark(nc.vector, S, ("r", sA, j))
                        V.ins(lambda j=j: nc.vector.tensor_tensor(ub[:, :], ub[:, :],
                                                                  rgate[:, j, :], op=OP.mult),
                              mark=("u1", sA, j))
                        P.wait_mark(nc.vector, S, ("e", sA, j))
                        V.ins(lambda j=j: nc.vector.tensor_tensor(ub[:, :], ub[:, :],
                                                                  egel[:, j % 4, :], op=OP.mult),
                              mark=("vp", sA, j))
                        V.ins(lambda j=j: nc.vector.tensor_tensor_scan(
                            ytmp[:, j % 2, :], a_t[:, j, :], ub[:, :], ycar[:, j:j + 1],
                            op0=OP.mult, op1=OP.add), mark=("scanop", sA, j))
                        if j == 1 and doW:
                            P.wait_mark(nc.vector, TE, ("wout_done", s))
                        if j >= 1:
                            V.ins(lambda j=j: nc.vector.tensor_copy(
                                ycar[:, j - 1:j], ytmp[:, (j - 1) % 2, SIG - 1:SIG]))
                            V.ins(lambda j=j: nc.vector.tensor_copy(
                                ybf[:, j - 1, :], ytmp[:, (j - 1) % 2, :]))
                    P.wait_mark(nc.vector, V, ("scanop", sA, KC - 1))
                    V.ins(lambda: nc.vector.tensor_copy(ycar[:, KC - 1:KC],
                                                        ytmp[:, (KC - 1) % 2, SIG - 1:SIG]))
                    V.ins(lambda: nc.vector.tensor_copy(ybf[:, KC - 1, :], ytmp[:, (KC - 1) % 2, :]),
                          mark=("y", sA))
                    V.mark_here(("scan", sA))

                if doL:
                    dwait(nc.vector, s_ldx, c["ldx"])
                    for tt in range(NTT):
                        slot = (sL * NTT + tt) % NSLOT
                        V.ins(lambda slot=slot, tt=tt: nc.vector.bn_stats(
                            stat[:, tt * 12:tt * 12 + 6], reg[:, slot, 0:512]))
                        V.ins(lambda slot=slot, tt=tt: nc.vector.bn_stats(
                            stat[:, tt * 12 + 6:tt * 12 + 12], reg[:, slot, 512:1024]))
                    ln_aggr_norm("L1", sL, sc2, 1.000001, 1e-11, extra_r1=True)
                    for tt in range(NTT):
                        slot = (sL * NTT + tt) % NSLOT
                        dwait(nc.vector, s_dmat, c["dmat"])
                        V.ins(lambda slot=slot, tt=tt: nc.vector.tensor_scalar(
                            stok[:, :], reg[:, slot, :], sc2[:, tt:tt + 1], msc[:, tt:tt + 1],
                            op0=OP.mult, op1=OP.subtract), mark=("ln1", sL, tt))
                        V.ins(lambda slot=slot, tt=tt: nc.vector.tensor_scalar(
                            reg[:, slot, :], reg[:, slot, :], sct[:, tt:tt + 1],
                            msc[:, NTT + tt:NTT + tt + 1], op0=OP.mult, op1=OP.subtract))

                        P.wait_mark(nc.sync, V, ("ln1", sL, tt))
                        if sL >= 1:
                            P.wait_mark(nc.sync, TE, ("proj_done", sL - 1))
                        dma(nc.sync, s_dmat, "dmat",
                            lambda tt=tt: xnf[:, :, tt * 128:(tt + 1) * 128], lambda: stok[:, :],
                            transpose=True)
                        dwait(nc.sync, s_dmat, c["dmat"])
                    xn_dmat_at[sL] = c["dmat"]
                if s == 0:
                    for kc in range(KC):
                        dwait(nc.vector, s_ldg, 64 + 16 * (kc + 1))
                        V.ins(lambda kc=kc: nc.vector.tensor_scalar(
                            w_in[:, kc, :], w_in[:, kc, :], vc(B_WN, kc), None, op0=OP.mult),
                            mark=("fold", kc))
                if s == 1:
                    dwait(nc.sync, s_ldg, ldg_all)
                    for scr, dst in [(wr_s, wrt), (wk_s, wkt), (wv_s, wvt)]:
                        dma(nc.sync, s_dmat, "dmat", lambda dst=dst: dst[:, :, :],
                            lambda scr=scr: scr[:, :], transpose=True)
                        dwait(nc.sync, s_dmat, c["dmat"])
                    wt_dmat[0] = c["dmat"]

                if doF:
                    if sF >= 1:
                        P.wait_mark(nc.scalar, TE, ("v_done", s - 1))
                    for j in range(KC):
                        P.wait_mark(nc.scalar, TE, ("pa", s, 24 + j))
                        b = pa_bank[(s, 24 + j)]
                        S.ins(lambda j=j, b=b: nc.scalar.activation(
                            e1b[:, j % 2, :], pa[b][:, :], AF.Relu), mark=("pafree", s, 24 + j))
                        S.ins(lambda j=j: nc.scalar.activation(
                            kact[:, j, :], e1b[:, j % 2, :], AF.Square), mark=("kact", sF, j))
                    for H in range(4):
                        P.wait_mark(nc.scalar, TE, ("prv_r", s, H))
                        if 4 * sF + H >= 3:
                            hp, hh = divmod(4 * sF + H - 3, 4)
                            P.wait_mark(nc.scalar, V, ("rv", hp, hh))
                        b = prv_bank[(s, "r", H)]
                        S.ins(lambda H=H, b=b: nc.scalar.activation(
                            rsig[:, H % 3, :], prv[b][:, :], AF.Sigmoid), mark=("prvfree_r", s, H))
                        S.mark_here(("rs", sF, H))

                if doW:
                    dwait(nc.vector, s_dmat, c["dmat"])
                    if doF:
                        P.wait_mark(nc.vector, TE, ("k_done", s))
                        P.wait_mark(nc.vector, TE, ("r_done", s))
                    par = sW % 2
                    for j in range(KC):
                        xv = xn2f[:, par, j, :]
                        V.ins(lambda xv=xv, j=j: nc.vector.tensor_tensor(
                            xdb[:, j % 2, 1:SIG], xv[:, 1:SIG], xv[:, 0:SIG - 1], op=OP.subtract))
                        V.ins(lambda xv=xv, j=j, par=par: nc.vector.tensor_tensor(
                            xdb[:, j % 2, 0:1], xv[:, 0:1], xn2f[:, 1 - par, j, SIG - 1:SIG],
                            op=OP.subtract))
                        if j >= 1:
                            V.ins(lambda j=j, par=par: nc.vector.scalar_tensor_tensor(
                                xrf[:, j - 1, :], xdb[:, (j - 1) % 2, :], vc(B_MUR1, j - 1),
                                xn2f[:, par, j - 1, :], op0=OP.mult, op1=OP.add))
                            V.ins(lambda j=j, par=par: nc.vector.scalar_tensor_tensor(
                                xkf[:, j - 1, :], xdb[:, (j - 1) % 2, :], vc(B_MUK1, j - 1),
                                xn2f[:, par, j - 1, :], op0=OP.mult, op1=OP.add))
                    V.ins(lambda par=par: nc.vector.scalar_tensor_tensor(
                        xrf[:, KC - 1, :], xdb[:, (KC - 1) % 2, :], vc(B_MUR1, KC - 1),
                        xn2f[:, par, KC - 1, :], op0=OP.mult, op1=OP.add))
                    V.ins(lambda par=par: nc.vector.scalar_tensor_tensor(
                        xkf[:, KC - 1, :], xdb[:, (KC - 1) % 2, :], vc(B_MUK1, KC - 1),
                        xn2f[:, par, KC - 1, :], op0=OP.mult, op1=OP.add), mark=("mix", sW))

                if doF:
                    for H in range(4):
                        tc, half = divmod(H, 2)
                        slot = (sF * NTT + tc) % NSLOT
                        P.wait_mark(nc.vector, TE, ("prv_v", s, H))
                        P.wait_mark(nc.vector, S, ("rs", sF, H))
                        b = prv_bank[(s, "v", H)]
                        rvv = ytmp[:, :, :].rearrange("p a b -> p (a b)")
                        V.ins(lambda b=b, H=H, rvv=rvv: nc.vector.tensor_tensor(
                            rvv, prv[b][:, :], rsig[:, H % 3, :], op=OP.mult))
                        V.ins(lambda slot=slot, half=half, rvv=rvv: nc.vector.tensor_tensor(
                            reg[:, slot, half * 512:(half + 1) * 512],
                            reg[:, slot, half * 512:(half + 1) * 512], rvv,
                            op=OP.add), mark=("rv", sF, H))
                        V.mark_here(("prvfree_v", s, H))
                        if half == 1:
                            V.mark_here(("x2", sF, tc))

                if doF:
                    for tt in range(NTT):
                        gt = sF * NTT + tt
                        P.wait_mark(nc.sync, V, ("x2", sF, tt))
                        dma(nc.sync, s_st, "st", lambda gt=gt: out_d[gt * 128:(gt + 1) * 128, :],
                            lambda gt=gt: reg[:, gt % NSLOT, :])
                    dwait(nc.sync, s_st, c["st"])
                if s + 1 < NS:
                    for tt in range(NTT):
                        gt = (s + 1) * NTT + tt
                        dma(nc.sync, s_ldx, "ldx", lambda gt=gt: reg[:, gt % NSLOT, :],
                            lambda gt=gt: xd[gt * 128:(gt + 1) * 128, :])
                    dwait(nc.sync, s_ldx, c["ldx"])

            dwait(nc.sync, s_st, 16 * 2 * NS)

        emit_program()
        P.finalize_pass1()
        TE.sem, V.sem, S.sem, G.sem = sem_T, sem_V, sem_S, sem_G
        emit_program()
    return nc


def kernel(L0_attn_x, L0_attn_w_norm, L0_attn_W_in, L0_attn_lambda_log, L0_attn_W_out,
           L0_ffn_W_r, L0_ffn_W_k, L0_ffn_W_v, L0_ffn_mu_r, L0_ffn_mu_k):
    B, T, D_ = L0_attn_x.shape
    assert (B, T, D_) == (8, 2048, 1024)
    nc = build(T)
    w = {
        "w_in": np.ascontiguousarray(L0_attn_W_in, dtype=np.float32),
        "w_out": np.ascontiguousarray(L0_attn_W_out, dtype=np.float32),
        "w_r": np.ascontiguousarray(L0_ffn_W_r, dtype=np.float32),
        "w_k": np.ascontiguousarray(L0_ffn_W_k, dtype=np.float32),
        "w_v": np.ascontiguousarray(L0_ffn_W_v, dtype=np.float32),
        "wn": np.ascontiguousarray(L0_attn_w_norm, dtype=np.float32),
        "ll": np.ascontiguousarray(L0_attn_lambda_log, dtype=np.float32),
        "mur": np.ascontiguousarray(L0_ffn_mu_r, dtype=np.float32),
        "muk": np.ascontiguousarray(L0_ffn_mu_k, dtype=np.float32),
    }
    in_maps = [dict(w, x=np.ascontiguousarray(L0_attn_x[i], dtype=np.float32)) for i in range(B)]
    res = run_bass_kernel_spmd(nc, in_maps, core_ids=list(range(B)))
    return np.stack([res.results[i]["out"] for i in range(B)], axis=0)


# version 18
# speedup vs baseline: 1.5676x; 1.0259x over previous
import numpy as np
from contextlib import ExitStack

import concourse.bass as bass
import concourse.mybir as mybir
from concourse.alu_op_type import AluOpType as OP
from concourse.bass_utils import run_bass_kernel_spmd

F32 = mybir.dt.float32
BF16 = mybir.dt.bfloat16
I32 = mybir.dt.int32
AF = mybir.ActivationFunctionType

D = 1024
KC = 8
SIG = 256
NTT = SIG // 128
NSLOT = 8
NWID = 64


class Eng:
    def __init__(self, prog, name):
        self.prog = prog
        self.name = name
        self.n = 0
        self.marks = {}
        self.sem = None

    def ins(self, f, mark=None):
        self.n += 1
        if self.prog.emit:
            instr = f()
            if self.n in self.prog.signals[self.name]:
                instr.then_inc(self.sem, 1)
        if mark is not None:
            if self.prog.emit:
                assert self.marks.get(mark) == self.n, \
                    f"pass drift: {self.name} {mark}: {self.marks.get(mark)} vs {self.n}"
            self.marks[mark] = self.n
        return self.n

    def mark_here(self, key):
        if self.prog.emit:
            assert self.marks.get(key) == self.n
        self.marks[key] = self.n


class Prog:
    def __init__(self):
        self.emit = False
        self.engs = {}
        self.waited_keys = {n: set() for n in ("T", "V", "S", "G")}
        self.signals = {n: {} for n in ("T", "V", "S", "G")}
        self.dma = {}

    def eng(self, name):
        if name not in self.engs:
            self.engs[name] = Eng(self, name)
        return self.engs[name]

    def wait_mark(self, waiter_raw, producer, key):
        if not self.emit:
            self.waited_keys[producer.name].add(key)
            return
        idx = producer.marks[key]
        waiter_raw.wait_ge(producer.sem, self.signals[producer.name][idx])

    def finalize_pass1(self):
        for name, keys in self.waited_keys.items():
            marks = self.engs[name].marks
            idxs = sorted({marks[k] for k in keys})
            self.signals[name] = {i: r + 1 for r, i in enumerate(idxs)}
        for e in self.engs.values():
            e.n = 0
        self.dma = {}
        self.emit = True


def _newton_rsqrt(V, nc, out, vin, nw, add_c, mul_c, q, wide, tmp, iters=4, mark=None):
    if mul_c != 1.0:
        V.ins(lambda: nc.vector.tensor_scalar(q[:, :], vin, mul_c, add_c,
                                              op0=OP.mult, op1=OP.add))
    else:
        V.ins(lambda: nc.vector.tensor_scalar(q[:, :], vin, add_c, None, op0=OP.add))
    qi, oi = q[:, :].bitcast(I32), wide[:, :].bitcast(I32)
    V.ins(lambda: nc.vector.tensor_scalar(qi, qi, 0x7fffffff, None, op0=OP.bitwise_and))
    V.ins(lambda: nc.vector.tensor_scalar(oi, qi, 1, None, op0=OP.arith_shift_right))
    V.ins(lambda: nc.vector.tensor_scalar(oi, oi, -1, None, op0=OP.bitwise_xor))
    V.ins(lambda: nc.vector.tensor_scalar(oi, oi, 0x5f3759df + 1, None, op0=OP.add))
    for _ in range(iters):
        V.ins(lambda: nc.vector.tensor_tensor(tmp[:, :], wide[:, :], wide[:, :], op=OP.mult))
        V.ins(lambda: nc.vector.tensor_tensor(tmp[:, :], tmp[:, :], q[:, :], op=OP.mult))
        V.ins(lambda: nc.vector.tensor_scalar(tmp[:, :], tmp[:, :], -0.5, 1.5,
                                              op0=OP.mult, op1=OP.add))
        V.ins(lambda: nc.vector.tensor_tensor(wide[:, :], wide[:, :], tmp[:, :], op=OP.mult))
    V.ins(lambda: nc.vector.tensor_copy(out, wide[:, 0:nw]), mark=mark)


def build(T):
    NS = T // SIG
    nc = bass.Bass("TRN2", target_bir_lowering=False, detect_race_conditions=False)

    xd = nc.dram_tensor("x", [T, D], F32, kind="ExternalInput").ap()
    win_d = nc.dram_tensor("w_in", [D, 3 * D], F32, kind="ExternalInput").ap()
    wout_d = nc.dram_tensor("w_out", [D, D], F32, kind="ExternalInput").ap()
    wr_d = nc.dram_tensor("w_r", [D, D], F32, kind="ExternalInput").ap()
    wk_d = nc.dram_tensor("w_k", [D, D], F32, kind="ExternalInput").ap()
    wv_d = nc.dram_tensor("w_v", [D, D], F32, kind="ExternalInput").ap()
    wn_d = nc.dram_tensor("wn", [D], F32, kind="ExternalInput").ap()
    ll_d = nc.dram_tensor("ll", [D], F32, kind="ExternalInput").ap()
    mur_d = nc.dram_tensor("mur", [D], F32, kind="ExternalInput").ap()
    muk_d = nc.dram_tensor("muk", [D], F32, kind="ExternalInput").ap()
    out_d = nc.dram_tensor("out", [T, D], F32, kind="ExternalOutput").ap()
    wr_s = nc.dram_tensor("wr_s", [D, D], BF16, kind="Internal").ap()
    wk_s = nc.dram_tensor("wk_s", [D, D], BF16, kind="Internal").ap()
    wv_s = nc.dram_tensor("wv_s", [D, D], BF16, kind="Internal").ap()

    with ExitStack() as ctx:
        def sb(name, shape, dtype):
            return ctx.enter_context(nc.sbuf_tensor(name, shape, dtype))

        def psum_(name, shape):
            return ctx.enter_context(nc.psum_tensor(name, shape, F32))

        reg = sb("reg", [128, NSLOT, D], F32)
        w_in = sb("w_in_sb", [128, KC, 3 * D], BF16)
        w_out = sb("w_out_sb", [128, KC, D], BF16)
        wrt = sb("wrt", [128, KC, D], BF16)
        wkt = sb("wkt", [128, KC, D], BF16)
        wvt = sb("wvt", [128, KC, D], BF16)
        xnf = sb("xnf", [128, KC, SIG], BF16)
        xn2f = sb("xn2f", [128, 2, KC, SIG], BF16)
        stok = sb("stok", [128, D], BF16)
        a_t = sb("a_t", [128, KC, SIG], F32)
        rgate = sb("rgate", [128, KC, SIG], F32)
        egel = sb("egel", [128, 3, SIG], F32)
        ub = sb("ub", [128, SIG], F32)
        ytmp = sb("ytmp", [128, 2, SIG], F32)
        xdb = [ytmp[:, 0, 0:128].bitcast(BF16), ytmp[:, 0, 128:256].bitcast(BF16)]
        ybf = sb("ybf", [128, KC, SIG], BF16)
        xrf = sb("xrf", [128, KC, SIG], BF16)
        xkf = sb("xkf", [128, KC, SIG], BF16)
        e1b = sb("e1b", [128, 2, SIG], F32)
        kact = sb("kact", [128, KC, SIG], BF16)
        rsig = sb("rsig", [128, 4, 512], F32)
        vecs = sb("vecs", [128, 24], F32)
        stat = ub[:, 0:24]
        mv = ub[:, 24:28].rearrange("p (a b) -> p a b", b=2)
        vvv = sb("vvv", [128, NWID], F32)
        ycar = vvv[:, 24:32]
        msc = vvv[:, 8:12]
        sc1 = vvv[:, 12:12 + NTT]
        sc2 = vvv[:, 16:16 + NTT]
        sc3 = vvv[:, 20:20 + NTT]
        scq = sb("scq", [128, NWID], F32)
        scw = sb("scw", [128, NWID], F32)
        sct = sb("sct", [128, NWID], F32)

        def vc(base, j):
            return vecs[:, base + j:base + j + 1]
        B_LAMH, B_MUR1, B_MUK1 = 0, 8, 16
        nlamh = vvv[:, 32:40]

        pa = [psum_("pa0", [128, SIG]), psum_("pa1", [128, SIG])]
        po = [psum_("po0", [128, 512]), psum_("po1", [128, 512]), psum_("po2", [128, 512])]
        prv = [psum_("prv0", [128, 512]), psum_("prv1", [128, 512]), psum_("prv2", [128, 512])]

        s_ldx = ctx.enter_context(nc.semaphore("s_ldx"))
        s_dmat = ctx.enter_context(nc.semaphore("s_dmat"))
        s_st = ctx.enter_context(nc.semaphore("s_st"))
        s_ldg = ctx.enter_context(nc.semaphore("s_ldg"))
        sem_T = ctx.enter_context(nc.semaphore("sem_T"))
        sem_G = ctx.enter_context(nc.semaphore("sem_G"))
        sem_V = ctx.enter_context(nc.semaphore("sem_V"))
        sem_S = ctx.enter_context(nc.semaphore("sem_S"))

        P = Prog()
        TE, V, S, G = P.eng("T"), P.eng("V"), P.eng("S"), P.eng("G")

        pa_bank = {}
        po_bank = {}
        prv_bank = {}
        npa = npo = nprv = 0
        for s in range(NS + 3):
            sA, sW, sF = s - 1, s - 2, s - 3
            if 0 <= sW < NS:
                for H in range(4):
                    po_bank[(s, H)] = npo % 3
                    npo += 1
            if 0 <= sA < NS:
                for m in range(24):
                    pa_bank[(s, m)] = npa % 2
                    npa += 1
            if 0 <= sF < NS:
                for j in range(KC):
                    pa_bank[(s, 24 + j)] = npa % 2
                    npa += 1
                for H in range(4):
                    prv_bank[(s, "r", H)] = nprv % 3
                    nprv += 1
                for H in range(4):
                    prv_bank[(s, "v", H)] = nprv % 3
                    nprv += 1

        def emit_program():
            em = P.emit
            c = P.dma
            for k in ("ldx", "dmat", "st", "ldg"):
                c[k] = 0
            pa_free = []
            po_free = []
            prv_free = []
            xn_dmat_at = {}
            wt_dmat = [0]

            def dma(eng_raw, sem, key, out_ap_f, in_ap_f, transpose=False):
                c[key] += 16
                if em:
                    eng_raw.dma_start(out_ap_f(), in_ap_f(), transpose=transpose).then_inc(sem, 16)

            def dwait(eng_raw, sem, val):
                if em and val > 0:
                    eng_raw.wait_ge(sem, val)

            def gate(eng_raw, lst, nbank):
                g = len(lst)
                if g >= nbank:
                    who, key = lst[g - nbank]
                    P.wait_mark(eng_raw, who, key)

            if em:
                nc.gpsimd.dma_start(a_t[:, 0, 0:8], wn_d.rearrange("(a b) -> b a", b=128),
                                    allow_slow_non_contiguous=True).then_inc(s_ldg, 16)
                for dvec, base in [(ll_d, B_LAMH), (mur_d, B_MUR1), (muk_d, B_MUK1)]:
                    nc.gpsimd.dma_start(vecs[:, base:base + 8],
                                        dvec.rearrange("(a b) -> b a", b=128),
                                        allow_slow_non_contiguous=True).then_inc(s_ldg, 16)
            c["ldg"] += 64
            for kc in range(KC):
                dma(nc.gpsimd, s_ldg, "ldg", lambda kc=kc: w_in[:, kc, :],
                    lambda kc=kc: win_d[kc * 128:(kc + 1) * 128, :])
            ldg_win = c["ldg"]
            dwait(nc.gpsimd, s_dmat, 32)
            for ec in range(KC):
                dma(nc.gpsimd, s_ldg, "ldg", lambda ec=ec: w_out[:, ec, :],
                    lambda ec=ec: wout_d[ec * 128:(ec + 1) * 128, :])
            ldg_wout = c["ldg"]
            dma(nc.gpsimd, s_ldg, "ldg", lambda: wr_s[:, :], lambda: wr_d[:, :])
            dma(nc.gpsimd, s_ldg, "ldg", lambda: wk_s[:, :], lambda: wk_d[:, :])
            dma(nc.gpsimd, s_ldg, "ldg", lambda: wv_s[:, :], lambda: wv_d[:, :])
            ldg_all = c["ldg"]

            dwait(nc.scalar, s_ldg, 64)
            S.ins(lambda: nc.scalar.activation(vecs[:, B_LAMH:B_LAMH + 8],
                                               vecs[:, B_LAMH:B_LAMH + 8], AF.Sigmoid), mark="lam")

            V.ins(lambda: nc.vector.memset(vvv[:, :], 1.0))
            P.wait_mark(nc.vector, S, "lam")
            V.ins(lambda: nc.vector.tensor_scalar(nlamh, vecs[:, B_LAMH:B_LAMH + 8],
                                                  -1.0, None, op0=OP.mult))
            V.ins(lambda: nc.vector.tensor_scalar(vecs[:, B_MUR1:B_MUR1 + 8],
                                                  vecs[:, B_MUR1:B_MUR1 + 8], -1.0, None, op0=OP.add))
            V.ins(lambda: nc.vector.tensor_scalar(vecs[:, B_MUK1:B_MUK1 + 8],
                                                  vecs[:, B_MUK1:B_MUK1 + 8], -1.0, None, op0=OP.add))
            V.ins(lambda: nc.vector.memset(ycar[:, :], 0.0))
            V.ins(lambda: nc.vector.memset(scq[:, :], 1.0))
            V.ins(lambda: nc.vector.memset(scw[:, :], 1.0))
            V.ins(lambda: nc.vector.memset(sct[:, :], 1.0))
            for j in range(KC - 1):
                V.ins(lambda j=j: nc.vector.memset(xn2f[:, 1, j, SIG - 1:SIG], 0.0))
            V.ins(lambda: nc.vector.memset(xn2f[:, 1, KC - 1, SIG - 1:SIG], 0.0), mark="prep")
            P.wait_mark(nc.tensor, V, ("fold", KC - 1))

            def ln_aggr_norm(tag, sg, scale_out, mul_c, add_c, extra_r1):
                for tt in range(NTT):
                    V.ins(lambda: nc.vector.memset(scq[:, 0:1], 0.0))
                    V.ins(lambda tt=tt: nc.vector.bn_aggr(mv[:, tt, :],
                                                          stat[:, tt * 12:tt * 12 + 12]),
                          mark=(tag + "ag", sg, tt))
                P.wait_mark(nc.vector, V, (tag + "ag", sg, NTT - 1))
                for tt in range(NTT):
                    V.ins(lambda tt=tt: nc.vector.tensor_copy(vvv[:, tt:tt + 1], mv[:, tt, 1:2]),
                          mark=(tag + "cp", sg, tt))
                P.wait_mark(nc.vector, V, (tag + "cp", sg, NTT - 1))
                if extra_r1:
                    _newton_rsqrt(V, nc, sc1, vvv[:, :], NTT, 1e-5, 1.0, scq, scw, sct)
                _newton_rsqrt(V, nc, scale_out[:, :], vvv[:, :], NTT, add_c, mul_c, scq, scw, sct,
                              mark=(tag + "nw", sg))
                P.wait_mark(nc.vector, V, (tag + "nw", sg))
                V.ins(lambda: nc.vector.tensor_tensor(msc[:, 0:NTT], mv[:, :, 0], scale_out[:, :],
                                                      op=OP.mult))
                if extra_r1:
                    V.ins(lambda: nc.vector.tensor_tensor(msc[:, NTT:2 * NTT], mv[:, :, 0],
                                                          sc1, op=OP.mult))
                    V.ins(lambda: nc.vector.tensor_scalar(sct[:, 0:NTT], sc1, 1.0, None,
                                                          op0=OP.add), mark=(tag + "ms", sg))
                else:
                    V.ins(lambda: nc.vector.tensor_copy(msc[:, 2:3], msc[:, 0:1]),
                          mark=(tag + "ms", sg))
                P.wait_mark(nc.vector, V, (tag + "ms", sg))

            for s in range(NS + 3):
                sL, sA, sW, sF = s, s - 1, s - 2, s - 3
                doL = sL < NS
                doA = 0 <= sA < NS
                doW = 0 <= sW < NS
                doF = 0 <= sF < NS

                if s == 0:
                    for tt in range(NTT):
                        gt = tt
                        dma(nc.sync, s_ldx, "ldx", lambda gt=gt: reg[:, gt % NSLOT, :],
                            lambda gt=gt: xd[gt * 128:(gt + 1) * 128, :])
                    dwait(nc.sync, s_ldx, c["ldx"])

                if doW:
                    if sW == 0:
                        dwait(nc.tensor, s_ldg, ldg_wout)
                    P.wait_mark(nc.tensor, V, ("y", sW))
                    for tc in range(NTT):
                        for half in range(2):
                            H = 2 * tc + half
                            gate(nc.tensor, po_free, 3)
                            po_free.append((V, ("pofree", s, H)))
                            b = po_bank[(s, H)]
                            for ec in range(KC):
                                TE.ins(lambda tc=tc, half=half, ec=ec, b=b: nc.tensor.matmul(
                                    po[b][:, :], ybf[:, ec, tc * 128:(tc + 1) * 128],
                                    w_out[:, ec, half * 512:(half + 1) * 512],
                                    start=(ec == 0), stop=(ec == KC - 1)),
                                    mark=(("po", s, H) if ec == KC - 1 else None))
                    TE.mark_here(("wout_done", s))

                if doA:
                    dwait(nc.tensor, s_dmat, xn_dmat_at[sA])
                    for m in range(24):
                        gate(nc.tensor, pa_free, 2)
                        pa_free.append((S, ("pafree", s, m)))
                        b = pa_bank[(s, m)]
                        for kc in range(KC):
                            TE.ins(lambda kc=kc, m=m, b=b: nc.tensor.matmul(
                                pa[b][:, :], w_in[:, kc, m * 128:(m + 1) * 128], xnf[:, kc, :],
                                start=(kc == 0), stop=(kc == KC - 1)),
                                mark=(("pa", s, m) if kc == KC - 1 else None))
                    TE.mark_here(("proj_done", sA))

                if doF:
                    if sF == 0:
                        dwait(nc.tensor, s_dmat, wt_dmat[0])
                    P.wait_mark(nc.tensor, V, ("mix", sF))
                    for j in range(KC):
                        gate(nc.tensor, pa_free, 2)
                        pa_free.append((S, ("pafree", s, 24 + j)))
                        b = pa_bank[(s, 24 + j)]
                        for cc in range(KC):
                            TE.ins(lambda cc=cc, j=j, b=b: nc.tensor.matmul(
                                pa[b][:, :], wkt[:, cc, j * 128:(j + 1) * 128], xkf[:, cc, :],
                                start=(cc == 0), stop=(cc == KC - 1)),
                                mark=(("pa", s, 24 + j) if cc == KC - 1 else None))
                    TE.mark_here(("k_done", s))
                    for tc in range(NTT):
                        for half in range(2):
                            H = 2 * tc + half
                            gate(nc.tensor, prv_free, 3)
                            prv_free.append((S, ("prvfree_r", s, H)))
                            b = prv_bank[(s, "r", H)]
                            for cc in range(KC):
                                TE.ins(lambda cc=cc, tc=tc, half=half, b=b: nc.tensor.matmul(
                                    prv[b][:, :], xrf[:, cc, tc * 128:(tc + 1) * 128],
                                    wrt[:, cc, half * 512:(half + 1) * 512],
                                    start=(cc == 0), stop=(cc == KC - 1)),
                                    mark=(("prv_r", s, H) if cc == KC - 1 else None))
                    TE.mark_here(("r_done", s))
                    P.wait_mark(nc.tensor, S, ("kact", sF, KC - 1))
                    for tc in range(NTT):
                        for half in range(2):
                            H = 2 * tc + half
                            gate(nc.tensor, prv_free, 3)
                            prv_free.append((V, ("prvfree_v", s, H)))
                            b = prv_bank[(s, "v", H)]
                            for dc in range(KC):
                                TE.ins(lambda dc=dc, tc=tc, half=half, b=b: nc.tensor.matmul(
                                    prv[b][:, :], kact[:, dc, tc * 128:(tc + 1) * 128],
                                    wvt[:, dc, half * 512:(half + 1) * 512],
                                    start=(dc == 0), stop=(dc == KC - 1)),
                                    mark=(("prv_v", s, H) if dc == KC - 1 else None))
                    TE.mark_here(("v_done", s))

                if doW:
                    for tc in range(NTT):
                        slot = (sW * NTT + tc) % NSLOT
                        for half in range(2):
                            H = 2 * tc + half
                            P.wait_mark(nc.vector, TE, ("po", s, H))
                            b = po_bank[(s, H)]
                            V.ins(lambda slot=slot, half=half, b=b: nc.vector.tensor_tensor(
                                reg[:, slot, half * 512:(half + 1) * 512],
                                reg[:, slot, half * 512:(half + 1) * 512],
                                po[b][:, :], op=OP.add), mark=("pofree", s, H))
                        V.ins(lambda slot=slot, tc=tc: nc.vector.bn_stats(
                            stat[:, tc * 12:tc * 12 + 6], reg[:, slot, 0:512]))
                        V.ins(lambda slot=slot, tc=tc: nc.vector.bn_stats(
                            stat[:, tc * 12 + 6:tc * 12 + 12], reg[:, slot, 512:1024]))
                    ln_aggr_norm("L2", sW, sc3, 1.0, 1e-5, extra_r1=False)
                    for tt in range(NTT):
                        slot = (sW * NTT + tt) % NSLOT
                        dwait(nc.vector, s_dmat, c["dmat"])
                        V.ins(lambda slot=slot, tt=tt: nc.vector.tensor_scalar(
                            stok[:, :], reg[:, slot, :], sc3[:, tt:tt + 1], msc[:, tt:tt + 1],
                            op0=OP.mult, op1=OP.subtract), mark=("ln2", sW, tt))

                        P.wait_mark(nc.sync, V, ("ln2", sW, tt))
                        if sW >= 1:
                            P.wait_mark(nc.sync, V, ("mix", sW - 1))
                        dma(nc.sync, s_dmat, "dmat",
                            lambda tt=tt, par=sW % 2: xn2f[:, par, :, tt * 128:(tt + 1) * 128],
                            lambda: stok[:, :], transpose=True)
                        dwait(nc.sync, s_dmat, c["dmat"])

                if doA:
                    if sA >= 1:
                        P.wait_mark(nc.scalar, V, ("scan", sA - 1))
                    for j in range(KC):
                        P.wait_mark(nc.scalar, TE, ("pa", s, j))
                        b = pa_bank[(s, j)]
                        S.ins(lambda j=j, b=b: nc.scalar.activation(
                            a_t[:, j, :], pa[b][:, :], AF.Sigmoid), mark=("pafree", s, j))
                        S.mark_here(("a", sA, j))
                    if sA >= 1:
                        P.wait_mark(nc.scalar, V, ("u1", sA - 1, KC - 1))
                    for j in range(KC):
                        P.wait_mark(nc.scalar, TE, ("pa", s, 8 + j))
                        b = pa_bank[(s, 8 + j)]
                        S.ins(lambda j=j, b=b: nc.scalar.activation(
                            rgate[:, j, :], pa[b][:, :], AF.Sigmoid), mark=("pafree", s, 8 + j))
                        S.mark_here(("r", sA, j))
                    for j in range(KC):
                        J = KC * sA + j
                        P.wait_mark(nc.scalar, TE, ("pa", s, 16 + j))
                        if J >= 3:
                            sp, jp = divmod(J - 3, KC)
                            P.wait_mark(nc.scalar, V, ("vp", sp, jp))
                        b = pa_bank[(s, 16 + j)]
                        S.ins(lambda j=j, b=b: nc.scalar.activation(
                            egel[:, j % 3, :], pa[b][:, :], AF.Gelu), mark=("pafree", s, 16 + j))
                        S.mark_here(("e", sA, j))

                if doA:
                    for j in range(KC):
                        P.wait_mark(nc.vector, S, ("a", sA, j))
                        V.ins(lambda j=j: nc.vector.tensor_scalar(ub[:, :], a_t[:, j, :],
                                                                  1.0, nlamh[:, j:j + 1],
                                                                  op0=OP.subtract, op1=OP.mult))
                        P.wait_mark(nc.vector, S, ("r", sA, j))
                        V.ins(lambda j=j: nc.vector.tensor_tensor(ub[:, :], ub[:, :],
                                                                  rgate[:, j, :], op=OP.mult),
                              mark=("u1", sA, j))
                        P.wait_mark(nc.vector, S, ("e", sA, j))
                        V.ins(lambda j=j: nc.vector.tensor_tensor(ub[:, :], ub[:, :],
                                                                  egel[:, j % 3, :], op=OP.mult),
                              mark=("vp", sA, j))
                        V.ins(lambda j=j: nc.vector.tensor_tensor_scan(
                            ytmp[:, j % 2, :], a_t[:, j, :], ub[:, :], ycar[:, j:j + 1],
                            op0=OP.mult, op1=OP.add), mark=("scanop", sA, j))
                        if j == 1 and doW:
                            P.wait_mark(nc.vector, TE, ("wout_done", s))
                        if j >= 1:
                            V.ins(lambda j=j: nc.vector.tensor_copy(
                                ycar[:, j - 1:j], ytmp[:, (j - 1) % 2, SIG - 1:SIG]))
                            V.ins(lambda j=j: nc.vector.tensor_copy(
                                ybf[:, j - 1, :], ytmp[:, (j - 1) % 2, :]))
                    P.wait_mark(nc.vector, V, ("scanop", sA, KC - 1))
                    V.ins(lambda: nc.vector.tensor_copy(ycar[:, KC - 1:KC],
                                                        ytmp[:, (KC - 1) % 2, SIG - 1:SIG]))
                    V.ins(lambda: nc.vector.tensor_copy(ybf[:, KC - 1, :], ytmp[:, (KC - 1) % 2, :]),
                          mark=("y", sA))
                    V.mark_here(("scan", sA))

                if doL:
                    dwait(nc.vector, s_ldx, c["ldx"])
                    for tt in range(NTT):
                        slot = (sL * NTT + tt) % NSLOT
                        V.ins(lambda slot=slot, tt=tt: nc.vector.bn_stats(
                            stat[:, tt * 12:tt * 12 + 6], reg[:, slot, 0:512]))
                        V.ins(lambda slot=slot, tt=tt: nc.vector.bn_stats(
                            stat[:, tt * 12 + 6:tt * 12 + 12], reg[:, slot, 512:1024]))
                    ln_aggr_norm("L1", sL, sc2, 1.000001, 1e-11, extra_r1=True)
                    for tt in range(NTT):
                        slot = (sL * NTT + tt) % NSLOT
                        dwait(nc.vector, s_dmat, c["dmat"])
                        V.ins(lambda slot=slot, tt=tt: nc.vector.tensor_scalar(
                            stok[:, :], reg[:, slot, :], sc2[:, tt:tt + 1], msc[:, tt:tt + 1],
                            op0=OP.mult, op1=OP.subtract), mark=("ln1", sL, tt))
                        V.ins(lambda slot=slot, tt=tt: nc.vector.tensor_scalar(
                            reg[:, slot, :], reg[:, slot, :], sct[:, tt:tt + 1],
                            msc[:, NTT + tt:NTT + tt + 1], op0=OP.mult, op1=OP.subtract))

                        P.wait_mark(nc.sync, V, ("ln1", sL, tt))
                        if sL >= 1:
                            P.wait_mark(nc.sync, TE, ("proj_done", sL - 1))
                        dma(nc.sync, s_dmat, "dmat",
                            lambda tt=tt: xnf[:, :, tt * 128:(tt + 1) * 128], lambda: stok[:, :],
                            transpose=True)
                        dwait(nc.sync, s_dmat, c["dmat"])
                    xn_dmat_at[sL] = c["dmat"]
                if s == 0:
                    for kc in range(KC):
                        dwait(nc.vector, s_ldg, 64 + 16 * (kc + 1))
                        V.ins(lambda kc=kc: nc.vector.tensor_scalar(
                            w_in[:, kc, :], w_in[:, kc, :], a_t[:, 0, kc:kc + 1], None,
                            op0=OP.mult), mark=("fold", kc))
                if s == 1:
                    dwait(nc.sync, s_ldg, ldg_all)
                    for scr, dst in [(wr_s, wrt), (wk_s, wkt), (wv_s, wvt)]:
                        dma(nc.sync, s_dmat, "dmat", lambda dst=dst: dst[:, :, :],
                            lambda scr=scr: scr[:, :], transpose=True)
                        dwait(nc.sync, s_dmat, c["dmat"])
                    wt_dmat[0] = c["dmat"]

                if doF:
                    if sF >= 1:
                        P.wait_mark(nc.scalar, TE, ("v_done", s - 1))
                    for j in range(KC):
                        P.wait_mark(nc.scalar, TE, ("pa", s, 24 + j))
                        b = pa_bank[(s, 24 + j)]
                        S.ins(lambda j=j, b=b: nc.scalar.activation(
                            e1b[:, j % 2, :], pa[b][:, :], AF.Relu), mark=("pafree", s, 24 + j))
                        S.ins(lambda j=j: nc.scalar.activation(
                            kact[:, j, :], e1b[:, j % 2, :], AF.Square), mark=("kact", sF, j))
                    for H in range(4):
                        P.wait_mark(nc.scalar, TE, ("prv_r", s, H))
                        if 4 * sF + H >= 4:
                            hp, hh = divmod(4 * sF + H - 4, 4)
                            P.wait_mark(nc.scalar, V, ("rv", hp, hh))
                        b = prv_bank[(s, "r", H)]
                        S.ins(lambda H=H, b=b: nc.scalar.activation(
                            rsig[:, H % 4, :], prv[b][:, :], AF.Sigmoid), mark=("prvfree_r", s, H))
                        S.mark_here(("rs", sF, H))

                if doW:
                    dwait(nc.vector, s_dmat, c["dmat"])
                    if doF:
                        P.wait_mark(nc.vector, TE, ("k_done", s))
                        P.wait_mark(nc.vector, TE, ("r_done", s))
                    par = sW % 2
                    for j in range(KC):
                        xv = xn2f[:, par, j, :]
                        V.ins(lambda xv=xv, j=j: nc.vector.tensor_tensor(
                            xdb[j % 2][:, 1:SIG], xv[:, 1:SIG], xv[:, 0:SIG - 1], op=OP.subtract))
                        V.ins(lambda xv=xv, j=j, par=par: nc.vector.tensor_tensor(
                            xdb[j % 2][:, 0:1], xv[:, 0:1], xn2f[:, 1 - par, j, SIG - 1:SIG],
                            op=OP.subtract))
                        if j >= 1:
                            V.ins(lambda j=j, par=par: nc.vector.scalar_tensor_tensor(
                                xrf[:, j - 1, :], xdb[(j - 1) % 2][:, :], vc(B_MUR1, j - 1),
                                xn2f[:, par, j - 1, :], op0=OP.mult, op1=OP.add))
                            V.ins(lambda j=j, par=par: nc.vector.scalar_tensor_tensor(
                                xkf[:, j - 1, :], xdb[(j - 1) % 2][:, :], vc(B_MUK1, j - 1),
                                xn2f[:, par, j - 1, :], op0=OP.mult, op1=OP.add))
                    V.ins(lambda par=par: nc.vector.scalar_tensor_tensor(
                        xrf[:, KC - 1, :], xdb[(KC - 1) % 2][:, :], vc(B_MUR1, KC - 1),
                        xn2f[:, par, KC - 1, :], op0=OP.mult, op1=OP.add))
                    V.ins(lambda par=par: nc.vector.scalar_tensor_tensor(
                        xkf[:, KC - 1, :], xdb[(KC - 1) % 2][:, :], vc(B_MUK1, KC - 1),
                        xn2f[:, par, KC - 1, :], op0=OP.mult, op1=OP.add), mark=("mix", sW))

                if doF:
                    for H in range(4):
                        tc, half = divmod(H, 2)
                        slot = (sF * NTT + tc) % NSLOT
                        P.wait_mark(nc.vector, TE, ("prv_v", s, H))
                        P.wait_mark(nc.vector, S, ("rs", sF, H))
                        b = prv_bank[(s, "v", H)]
                        rvv = ytmp[:, :, :].rearrange("p a b -> p (a b)")
                        V.ins(lambda b=b, H=H, rvv=rvv: nc.vector.tensor_tensor(
                            rvv, prv[b][:, :], rsig[:, H % 4, :], op=OP.mult))
                        V.ins(lambda slot=slot, half=half, rvv=rvv: nc.vector.tensor_tensor(
                            reg[:, slot, half * 512:(half + 1) * 512],
                            reg[:, slot, half * 512:(half + 1) * 512], rvv,
                            op=OP.add), mark=("rv", sF, H))
                        V.mark_here(("prvfree_v", s, H))
                        if half == 1:
                            V.mark_here(("x2", sF, tc))

                if doF:
                    for tt in range(NTT):
                        gt = sF * NTT + tt
                        P.wait_mark(nc.sync, V, ("x2", sF, tt))
                        dma(nc.sync, s_st, "st", lambda gt=gt: out_d[gt * 128:(gt + 1) * 128, :],
                            lambda gt=gt: reg[:, gt % NSLOT, :])
                    dwait(nc.sync, s_st, c["st"])
                if s + 1 < NS:
                    for tt in range(NTT):
                        gt = (s + 1) * NTT + tt
                        dma(nc.sync, s_ldx, "ldx", lambda gt=gt: reg[:, gt % NSLOT, :],
                            lambda gt=gt: xd[gt * 128:(gt + 1) * 128, :])
                    dwait(nc.sync, s_ldx, c["ldx"])

            dwait(nc.sync, s_st, 16 * 2 * NS)

        emit_program()
        P.finalize_pass1()
        TE.sem, V.sem, S.sem, G.sem = sem_T, sem_V, sem_S, sem_G
        emit_program()
    return nc


def kernel(L0_attn_x, L0_attn_w_norm, L0_attn_W_in, L0_attn_lambda_log, L0_attn_W_out,
           L0_ffn_W_r, L0_ffn_W_k, L0_ffn_W_v, L0_ffn_mu_r, L0_ffn_mu_k):
    B, T, D_ = L0_attn_x.shape
    assert (B, T, D_) == (8, 2048, 1024)
    nc = build(T)
    w = {
        "w_in": np.ascontiguousarray(L0_attn_W_in, dtype=np.float32),
        "w_out": np.ascontiguousarray(L0_attn_W_out, dtype=np.float32),
        "w_r": np.ascontiguousarray(L0_ffn_W_r, dtype=np.float32),
        "w_k": np.ascontiguousarray(L0_ffn_W_k, dtype=np.float32),
        "w_v": np.ascontiguousarray(L0_ffn_W_v, dtype=np.float32),
        "wn": np.ascontiguousarray(L0_attn_w_norm, dtype=np.float32),
        "ll": np.ascontiguousarray(L0_attn_lambda_log, dtype=np.float32),
        "mur": np.ascontiguousarray(L0_ffn_mu_r, dtype=np.float32),
        "muk": np.ascontiguousarray(L0_ffn_mu_k, dtype=np.float32),
    }
    in_maps = [dict(w, x=np.ascontiguousarray(L0_attn_x[i], dtype=np.float32)) for i in range(B)]
    res = run_bass_kernel_spmd(nc, in_maps, core_ids=list(range(B)))
    return np.stack([res.results[i]["out"] for i in range(B)], axis=0)
